# version 03
# baseline (speedup 1.0000x reference)
import contextlib
import numpy as np
import ml_dtypes
import concourse.bass as bass
import concourse.mybir as mybir
from concourse.bass import IndirectOffsetOnAxis
from concourse.bass_utils import run_bass_kernel_spmd

F32 = mybir.dt.float32
BF16 = mybir.dt.bfloat16
I32 = mybir.dt.int32
AF = mybir.ActivationFunctionType
ALU = mybir.AluOpType
AX = mybir.AxisListType

D = 2048
HID = 5632
NTOK = 2048
NCORE = 8
NHC = HID // 128
NDC = D // 128
TH = 1024
EPS_LN = 1e-5
EPS_RMS = 1e-6
NHEAD = 16
DH = 128
SFULL = 2 * NTOK

BF = ml_dtypes.bfloat16



def build_ffn():
    nc = bass.Bass("TRN2", target_bir_lowering=False, debug=False)
    xfm = nc.dram_tensor("xfm", [D, NTOK], F32, kind="ExternalInput").ap()
    wgT = nc.dram_tensor("wgT", [D, HID], I32, kind="ExternalInput").ap()
    wuT = nc.dram_tensor("wuT", [D, HID], I32, kind="ExternalInput").ap()
    wdT = nc.dram_tensor("wdT", [HID, D], I32, kind="ExternalInput").ap()
    sgT = nc.dram_tensor("sgT", [128, NHC], F32, kind="ExternalInput").ap()
    suT = nc.dram_tensor("suT", [128, NHC], F32, kind="ExternalInput").ap()
    sdT = nc.dram_tensor("sdT", [128, NDC], F32, kind="ExternalInput").ap()
    yfm = nc.dram_tensor("yfm", [D, NTOK], F32, kind="ExternalOutput").ap()
    zscr = nc.dram_tensor("zscr", [D, NTOK], F32).ap()
    with contextlib.ExitStack() as octx:
        def osem(name):
            return octx.enter_context(nc.semaphore(name))
        emit_ffn(nc, osem, "F", xfm, wgT, wuT, wdT, sgT, suT, sdT, yfm, zscr, [])
    return nc


FFN_FINAL = {
    "x_ld": 512, "sc_ld": 48, "wg0": 1408, "wg1": 1408, "wd0": 256, "wd1": 256,
    "gu_t": 88, "v_t1": 88, "gu_v": 88, "silu_a": 88, "a_v": 88, "dn_t": 32,
    "dn_v": 32, "z_v": 32, "q_a": 32, "st_t": 32, "xr_ld": 512, "zst": 512,
    "zl_ld": 512, "yst": 512, "t_ph": 12, "a_ph": 12, "ln_a": 2, "ln_v1": 6,
    "ln_a2": 2, "ln_v2": 2, "ap_v": 64, "ini_v": 1,
}


class _OffEng:

    def __init__(self, e, bases):
        self._e = e
        self._b = bases

    def wait_ge(self, s, v):
        return self._e.wait_ge(s, self._b.get(s, 0) + v)

    def __getattr__(self, n):
        return getattr(self._e, n)


def emit_ffn(nc, osem, pfx, xfm, wgT, wuT, wdT, sgT, suT, sdT, yfm, zscr, deps,
             sem_bank=None):
    created = {}
    bases = {}

    with contextlib.ExitStack() as ctx:
        def sb(name, shape, dt):
            return ctx.enter_context(nc.sbuf_tensor(pfx + name, shape, dt))

        def ps(name, shape, dt):
            return ctx.enter_context(nc.psum_tensor(pfx + name, shape, dt))

        def sem(name):
            if sem_bank is not None:
                h = sem_bank[name]
                bases[h] = FFN_FINAL[name]
                return h
            h = osem(pfx + name)
            created[name] = h
            return h

        def dw(eng):
            for s, v in deps:
                eng.wait_ge(s, v)

        xbf = sb("xbf", [128, NDC * TH], BF16)
        asb = sb("asb", [128, NHC * TH], BF16)
        warena = sb("warena", [128, 2 * NHC * 128], BF16)
        ones_c = sb("ones_c", [128, 1], BF16)
        ones_cf = sb("ones_cf", [128, 1], F32)
        ones_r = sb("ones_r", [1, 128], F32)
        sg_s = sb("sg_s", [128, NHC], F32)
        su_s = sb("su_s", [128, NHC], F32)
        sd_s = sb("sd_s", [128, NDC], F32)
        stat = sb("stat", [1, 6 * TH], F32)
        Bg = sb("Bg", [128, TH], F32)
        Ba = sb("Ba", [128, TH], F32)
        t1 = sb("t1", [128, TH], F32)
        t2 = sb("t2", [128, TH], BF16)
        t3 = sb("t3", [128, TH], F32)
        t4 = t1
        zsq = t3
        zsb = sb("zsb", [128, TH], F32)
        xres = sb("xres", [128, TH], F32)
        eps_t = sb("eps_t", [1, 1], F32)
        PA = ps("PA", [128, TH], F32)
        PB = ps("PB", [128, TH], F32)
        PC = ps("PC", [128, TH], F32)
        PD = ps("PD", [128, TH], F32)
        x_ld = sem("x_ld"); sc_ld = sem("sc_ld")
        wg_p = [sem("wg0"), sem("wg1")]; wd_p = [sem("wd0"), sem("wd1")]
        gu_t = sem("gu_t"); v_t1 = sem("v_t1"); gu_v = sem("gu_v"); silu_a = sem("silu_a")
        a_v = sem("a_v"); dn_t = sem("dn_t"); dn_v = sem("dn_v"); z_v = sem("z_v")
        q_a = sem("q_a"); st_t = sem("st_t"); xr_ld = sem("xr_ld"); zst = sem("zst")
        t_ph = sem("t_ph"); a_ph = sem("a_ph"); ln_a = sem("ln_a"); ln_v1 = sem("ln_v1")
        ln_a2 = sem("ln_a2"); ln_v2 = sem("ln_v2"); zl_ld = sem("zl_ld"); ap_v = sem("ap_v")
        yst = sem("yst"); ini_v = sem("ini_v")
        block = ctx.enter_context(nc.Block())
        WG0 = 0
        WG1 = NDC * 128
        WU0 = 2 * NDC * 128
        WU1 = 3 * NDC * 128
        WD0 = 0
        WD1 = NHC * 128
        CS = stat[0:1, 0:TH]
        M1 = stat[0:1, TH:2 * TH]
        M2 = stat[0:1, 2 * TH:3 * TH]
        VAR = stat[0:1, 3 * TH:4 * TH]
        RSTD = stat[0:1, 4 * TH:5 * TH]
        NMB = stat[0:1, 5 * TH:6 * TH]

        def xslice(dc, tt):
            c = dc * TH + tt * 512
            return xbf[:, c:c + 512]

        @block.gpsimd
        def _(g):
            dw(g)
            g = _OffEng(g, bases)
            for th in range(2):
                t0 = th * TH
                if th == 1:
                    g.wait_ge(gu_t, NHC)
                for dc in range(NDC):
                    g.dma_start(out=xbf[:, dc * TH:(dc + 1) * TH],
                                in_=xfm[dc * 128:(dc + 1) * 128, t0:t0 + TH]).then_inc(x_ld, 16)
                for hc in range(NHC):
                    it = th * NHC + hc
                    if th == 1 and hc == 0:
                        g.wait_ge(dn_t, NDC)
                    if it >= 2:
                        g.wait_ge(gu_t, it - 1)
                    c0 = hc * 128
                    wgdst = (WG0, WG1)[hc % 2]
                    wudst = (WU0, WU1)[hc % 2]
                    g.dma_start(
                        out=warena[:, wgdst:wgdst + NDC * 128],
                        in_=wgT[:, c0:c0 + 128].rearrange("(dc p) h -> p dc h", p=128),
                    ).then_inc(wg_p[hc % 2], 16)
                    g.dma_start(
                        out=warena[:, wudst:wudst + NDC * 128],
                        in_=wuT[:, c0:c0 + 128].rearrange("(dc p) h -> p dc h", p=128),
                    ).then_inc(wg_p[hc % 2], 16)
                for dt in range(NDC):
                    it = th * NDC + dt
                    if dt == 0:
                        g.wait_ge(gu_t, 44 * (th + 1))
                    if it >= 2:
                        g.wait_ge(dn_t, it - 1)
                    c0 = dt * 128
                    wddst = (WD0, WD1)[dt % 2]
                    g.dma_start(
                        out=warena[:, wddst:wddst + NHC * 128],
                        in_=wdT[:, c0:c0 + 128].rearrange("(hc p) h -> p hc h", p=128),
                    ).then_inc(wd_p[dt % 2], 16)

        @block.sync
        def _(sq):
            dw(sq)
            sq = _OffEng(sq, bases)
            sq.dma_start(out=sg_s[:, :], in_=sgT[:, :]).then_inc(sc_ld, 16)
            sq.dma_start(out=su_s[:, :], in_=suT[:, :]).then_inc(sc_ld, 16)
            sq.dma_start(out=sd_s[:, :], in_=sdT[:, :]).then_inc(sc_ld, 16)
            for th in range(2):
                t0 = th * TH
                for dt in range(NDC):
                    it = th * NDC + dt
                    if it >= 1:
                        sq.wait_ge(z_v, it)
                    sq.dma_start(
                        out=xres[:, :],
                        in_=xfm[dt * 128:(dt + 1) * 128, t0:t0 + TH]).then_inc(xr_ld, 16)
                    sq.wait_ge(z_v, it + 1)
                    sq.dma_start(
                        out=zscr[dt * 128:(dt + 1) * 128, t0:t0 + TH],
                        in_=zsb[:, :]).then_inc(zst, 16)
                for dt in range(NDC):
                    it = th * NDC + dt
                    if dt == 0:
                        sq.wait_ge(zst, 16 * 16 * (th + 1))
                    if it >= 1:
                        sq.wait_ge(ap_v, 2 * it - 1)
                    sq.dma_start(
                        out=t1[:, :],
                        in_=zscr[dt * 128:(dt + 1) * 128, t0:t0 + TH]).then_inc(zl_ld, 16)
                    sq.wait_ge(ap_v, 2 * (it + 1))
                    sq.dma_start(
                        out=yfm[dt * 128:(dt + 1) * 128, t0:t0 + TH],
                        in_=zsb[:, :]).then_inc(yst, 16)
            sq.wait_ge(yst, 16 * 32)

        @block.vector
        def _(V):
            dw(V)
            V = _OffEng(V, bases)
            V.memset(ones_c[:, :], 1.0)
            V.memset(ones_cf[:, :], 1.0)
            V.memset(ones_r[:, :], 1.0)
            V.memset(eps_t[:, :], EPS_LN).then_inc(ini_v, 1)
            V.wait_ge(sc_ld, 48)
            for th in range(2):
                for hc in range(NHC):
                    it = th * NHC + hc
                    psG = (PA, PC)[hc % 2]
                    psU = (PB, PD)[hc % 2]
                    V.wait_ge(gu_t, it + 1)
                    V.wait_ge(a_ph, 6 * th + 2)
                    if it >= 1:
                        V.wait_ge(silu_a, it)
                    V.tensor_tensor(out=t1[:, :], in0=psG[:, :], in1=Bg[:, :],
                                    op=ALU.add).then_inc(v_t1, 1)
                    if it >= 1:
                        V.wait_ge(a_v, it)
                    V.tensor_tensor(out=t3[:, :], in0=psU[:, :], in1=Bg[:, :],
                                    op=ALU.add).then_inc(gu_v, 1)
                    V.wait_ge(silu_a, it + 1)
                    V.wait_ge(gu_v, it + 1)
                    if th == 1:
                        V.wait_ge(dn_t, NDC)
                    V.scalar_tensor_tensor(
                        out=asb[:, hc * TH:(hc + 1) * TH], in0=t3[:, :],
                        scalar=su_s[:, hc:hc + 1], in1=t2[:, :],
                        op0=ALU.mult, op1=ALU.mult).then_inc(a_v, 1)
                for dt in range(NDC):
                    it = th * NDC + dt
                    psD = (PA, PB)[dt % 2]
                    V.wait_ge(dn_t, it + 1)
                    V.wait_ge(a_ph, 6 * th + 4)
                    V.wait_ge(silu_a, NHC * (th + 1))
                    if it >= 1:
                        V.wait_ge(z_v, it)
                    V.tensor_tensor(out=t4[:, :], in0=psD[:, :], in1=Ba[:, :],
                                    op=ALU.add).then_inc(dn_v, 1)
                    V.wait_ge(xr_ld, 16 * (it + 1))
                    if dt == 0 and th >= 1:
                        V.wait_ge(yst, 16 * 16 * th)
                    if it >= 1:
                        V.wait_ge(st_t, it)
                        V.wait_ge(q_a, it)
                        V.wait_ge(zst, 16 * it)
                    V.wait_ge(dn_v, it + 1)
                    V.scalar_tensor_tensor(
                        out=zsb[:, :], in0=t4[:, :],
                        scalar=sd_s[:, dt:dt + 1], in1=xres[:, :],
                        op0=ALU.mult, op1=ALU.add).then_inc(z_v, 1)
                V.wait_ge(ln_a, th + 1)
                V.tensor_tensor(out=VAR, in0=M1, in1=M1, op=ALU.mult).then_inc(ln_v1, 1)
                V.wait_ge(ln_v1, 3 * th + 1)
                V.tensor_tensor(out=VAR, in0=M2, in1=VAR, op=ALU.subtract).then_inc(ln_v1, 1)
                V.wait_ge(ln_a2, th + 1)
                V.reciprocal(out=VAR, in_=RSTD).then_inc(ln_v1, 1)
                V.wait_ge(ln_v1, 3 * th + 3)
                V.tensor_tensor(out=NMB, in0=M1, in1=VAR, op=ALU.mult).then_inc(ln_v2, 1)
                for dt in range(NDC):
                    it = th * NDC + dt
                    if dt == 0:
                        V.wait_ge(zst, 16 * 16 * (th + 1))
                    V.wait_ge(zl_ld, 16 * (it + 1))
                    V.wait_ge(a_ph, 6 * th + 6)
                    if it >= 1:
                        V.wait_ge(yst, 16 * it)
                        V.wait_ge(ap_v, 2 * it)
                    V.tensor_tensor(out=t3[:, :], in0=t1[:, :], in1=Bg[:, :],
                                    op=ALU.mult).then_inc(ap_v, 1)
                    V.wait_ge(ap_v, 2 * it + 1)
                    V.tensor_tensor(out=zsb[:, :],
                                    in0=t3[:, :], in1=Ba[:, :],
                                    op=ALU.subtract).then_inc(ap_v, 1)

        @block.scalar
        def _(A):
            dw(A)
            A = _OffEng(A, bases)
            A.wait_ge(sc_ld, 48)
            for th in range(2):
                A.wait_ge(t_ph, 6 * th + 1)
                A.activation(out=CS[:, 0:512], in_=PC[0:1, 0:512],
                             func=AF.Identity, scale=-128.0)
                A.activation(out=CS[:, 512:1024], in_=PC[0:1, 512:1024],
                             func=AF.Identity, scale=-128.0).then_inc(a_ph, 1)
                A.wait_ge(t_ph, 6 * th + 2)
                A.activation(out=Bg[:, 0:512], in_=PA[:, 0:512], func=AF.Identity)
                A.activation(out=Bg[:, 512:1024], in_=PA[:, 512:1024],
                             func=AF.Identity).then_inc(a_ph, 1)
                for hc in range(NHC):
                    it = th * NHC + hc
                    A.wait_ge(v_t1, it + 1)
                    if it >= 1:
                        A.wait_ge(a_v, it)
                    A.activation(out=t2[:, :], in_=t1[:, :], func=AF.Silu,
                                 scale=sg_s[:, hc:hc + 1]).then_inc(silu_a, 1)
                A.wait_ge(t_ph, 6 * th + 3)
                A.activation(out=CS[:, 0:512], in_=PA[0:1, 0:512],
                             func=AF.Identity, scale=-128.0)
                A.activation(out=CS[:, 512:1024], in_=PA[0:1, 512:1024],
                             func=AF.Identity, scale=-128.0).then_inc(a_ph, 1)
                A.wait_ge(t_ph, 6 * th + 4)
                A.activation(out=Ba[:, 0:512], in_=PB[:, 0:512], func=AF.Identity)
                A.activation(out=Ba[:, 512:1024], in_=PB[:, 512:1024],
                             func=AF.Identity).then_inc(a_ph, 1)
                for dt in range(NDC):
                    it = th * NDC + dt
                    A.wait_ge(z_v, it + 1)
                    A.wait_ge(a_v, NHC * (th + 1))
                    if it >= 1:
                        A.wait_ge(st_t, it)
                    A.activation(out=zsq[:, :], in_=zsb[:, :],
                                 func=AF.Square).then_inc(q_a, 1)
                A.wait_ge(st_t, 16 * (th + 1))
                A.activation(out=M1[:, 0:512], in_=PC[0:1, 0:512],
                             func=AF.Identity, scale=1.0 / D)
                A.activation(out=M1[:, 512:1024], in_=PC[0:1, 512:1024],
                             func=AF.Identity, scale=1.0 / D)
                A.activation(out=M2[:, 0:512], in_=PD[0:1, 0:512],
                             func=AF.Identity, scale=1.0 / D)
                A.activation(out=M2[:, 512:1024], in_=PD[0:1, 512:1024],
                             func=AF.Identity, scale=1.0 / D).then_inc(ln_a, 1)
                A.wait_ge(ln_v1, 3 * th + 2)
                A.wait_ge(ini_v, 1)
                A.activation(out=RSTD, in_=VAR, func=AF.Sqrt,
                             bias=eps_t[0:1, 0:1]).then_inc(ln_a2, 1)
                A.wait_ge(t_ph, 6 * th + 5)
                A.activation(out=Bg[:, 0:512], in_=PA[:, 0:512], func=AF.Identity)
                A.activation(out=Bg[:, 512:1024], in_=PA[:, 512:1024],
                             func=AF.Identity).then_inc(a_ph, 1)
                A.wait_ge(t_ph, 6 * th + 6)
                A.activation(out=Ba[:, 0:512], in_=PB[:, 0:512], func=AF.Identity)
                A.activation(out=Ba[:, 512:1024], in_=PB[:, 512:1024],
                             func=AF.Identity).then_inc(a_ph, 1)

        @block.tensor
        def _(T):
            dw(T)
            T = _OffEng(T, bases)
            T.wait_ge(ini_v, 1)
            for th in range(2):
                t0 = th * TH
                T.wait_ge(x_ld, 16 * NDC * (th + 1))
                if th == 1:
                    T.wait_ge(ln_a, 1)
                for tt in range(2):
                    for dc in range(NDC):
                        mm = T.matmul(PC[0:1, tt * 512:(tt + 1) * 512], ones_c[:, :],
                                      xslice(dc, tt),
                                      start=(dc == 0), stop=(dc == NDC - 1))
                mm.then_inc(t_ph, 1)
                T.wait_ge(a_ph, 6 * th + 1)
                if th == 1:
                    T.wait_ge(ap_v, 2 * 16 * th)
                for tt in range(2):
                    mm = T.matmul(PA[:, tt * 512:(tt + 1) * 512], ones_r[:, :],
                                  CS[:, tt * 512:(tt + 1) * 512], start=True, stop=True)
                mm.then_inc(t_ph, 1)
                for hc in range(NHC):
                    it = th * NHC + hc
                    psG = (PA, PC)[hc % 2]
                    psU = (PB, PD)[hc % 2]
                    T.wait_ge(wg_p[hc % 2], 32 * (it // 2 + 1))
                    if hc == 0:
                        T.wait_ge(a_ph, 6 * th + 2)
                    if it >= 2:
                        T.wait_ge(gu_v, it - 1)
                    wg = (WG0, WG1)[hc % 2]
                    wu = (WU0, WU1)[hc % 2]
                    for tt in range(2):
                        for dc in range(NDC):
                            T.matmul(psG[:, tt * 512:(tt + 1) * 512],
                                     warena[:, wg + dc * 128:wg + (dc + 1) * 128],
                                     xslice(dc, tt),
                                     start=(dc == 0), stop=(dc == NDC - 1))
                    for tt in range(2):
                        for dc in range(NDC):
                            mm = T.matmul(psU[:, tt * 512:(tt + 1) * 512],
                                          warena[:, wu + dc * 128:wu + (dc + 1) * 128],
                                          xslice(dc, tt),
                                          start=(dc == 0), stop=(dc == NDC - 1))
                    mm.then_inc(gu_t, 1)
                T.wait_ge(a_v, NHC * (th + 1))
                T.wait_ge(gu_v, NHC * (th + 1))
                for tt in range(2):
                    for hc in range(NHC):
                        mm = T.matmul(PA[0:1, tt * 512:(tt + 1) * 512], ones_c[:, :],
                                      asb[:, hc * TH + tt * 512: hc * TH + tt * 512 + 512],
                                      start=(hc == 0), stop=(hc == NHC - 1))
                mm.then_inc(t_ph, 1)
                T.wait_ge(a_ph, 6 * th + 3)
                for tt in range(2):
                    mm = T.matmul(PB[:, tt * 512:(tt + 1) * 512], ones_r[:, :],
                                  CS[:, tt * 512:(tt + 1) * 512], start=True, stop=True)
                mm.then_inc(t_ph, 1)
                for dt in range(NDC):
                    it = th * NDC + dt
                    psD = (PA, PB)[dt % 2]
                    T.wait_ge(wd_p[dt % 2], 16 * (it // 2 + 1))
                    if dt == 0:
                        T.wait_ge(a_ph, 6 * th + 4)
                    if it >= 2:
                        T.wait_ge(dn_v, it - 1)
                    wd = (WD0, WD1)[dt % 2]
                    for tt in range(2):
                        for hc in range(NHC):
                            mm = T.matmul(psD[:, tt * 512:(tt + 1) * 512],
                                          warena[:, wd + hc * 128:wd + (hc + 1) * 128],
                                          asb[:, hc * TH + tt * 512: hc * TH + tt * 512 + 512],
                                          start=(hc == 0), stop=(hc == NHC - 1))
                    mm.then_inc(dn_t, 1)
                    T.wait_ge(z_v, it + 1)
                    for tt in range(2):
                        T.matmul(PC[0:1, tt * 512:(tt + 1) * 512], ones_cf[:, :],
                                 zsb[:, tt * 512:tt * 512 + 512],
                                 start=(dt == 0), stop=(dt == NDC - 1))
                    T.wait_ge(q_a, it + 1)
                    for tt in range(2):
                        mm = T.matmul(PD[0:1, tt * 512:(tt + 1) * 512], ones_cf[:, :],
                                      zsq[:, tt * 512:(tt + 1) * 512],
                                      start=(dt == 0), stop=(dt == NDC - 1))
                    mm.then_inc(st_t, 1)
                T.wait_ge(ln_v2, th + 1)
                T.wait_ge(dn_v, 16 * (th + 1))
                for tt in range(2):
                    mm = T.matmul(PA[:, tt * 512:(tt + 1) * 512], ones_r[:, :],
                                  VAR[:, tt * 512:(tt + 1) * 512], start=True, stop=True)
                mm.then_inc(t_ph, 1)
                T.wait_ge(a_ph, 6 * th + 5)
                for tt in range(2):
                    mm = T.matmul(PB[:, tt * 512:(tt + 1) * 512], ones_r[:, :],
                                  NMB[:, tt * 512:(tt + 1) * 512], start=True, stop=True)
                mm.then_inc(t_ph, 1)

    return [(yst, bases.get(yst, 0) + 16 * 32)], created


def ffn_in_maps(xfm_list, wq_gate, s_gate, wq_up, s_up, wq_down, s_down):
    wgT = np.ascontiguousarray(np.asarray(wq_gate).T)
    wuT = np.ascontiguousarray(np.asarray(wq_up).T)
    wdT = np.ascontiguousarray(np.asarray(wq_down).T)
    sgT = np.ascontiguousarray(np.asarray(s_gate).reshape(NHC, 128).T)
    suT = np.ascontiguousarray(np.asarray(s_up).reshape(NHC, 128).T)
    sdT = np.ascontiguousarray(np.asarray(s_down).reshape(NDC, 128).T)
    return [
        {"xfm": x, "wgT": wgT, "wuT": wuT, "wdT": wdT,
         "sgT": sgT, "suT": suT, "sdT": sdT}
        for x in xfm_list
    ]



def build_conv(nslot):
    nc = bass.Bass("TRN2", target_bir_lowering=False, debug=False)
    x0 = nc.dram_tensor("x0", [NTOK, D], F32, kind="ExternalInput").ap()
    xsrc = nc.dram_tensor("xsrc", [nslot, 64, D], F32, kind="ExternalInput").ap()
    wsl = nc.dram_tensor("wsl", [nslot, 128, 512], F32, kind="ExternalInput").ap()
    idx = nc.dram_tensor("idx", [64, nslot], I32, kind="ExternalInput").ap()
    idf = nc.dram_tensor("idf", [128, 128], F32, kind="ExternalInput").ap()
    x1fm = nc.dram_tensor("x1fm", [D, NTOK], F32, kind="ExternalOutput").ap()
    zdr = nc.dram_tensor("zdr", [NTOK, D], F32).ap()
    with contextlib.ExitStack() as octx:
        def osem(name):
            return octx.enter_context(nc.semaphore(name))
        emit_conv(nc, osem, "C", nslot, x0, xsrc, wsl, idx, idf, x1fm, zdr, [])
    return nc


def emit_conv(nc, osem, pfx, nslot, x0, xsrc, wsl, idx, idf, x1fm, zdr, deps):
    XW = 2080

    with contextlib.ExitStack() as ctx:
        def sb(name, shape, dt):
            return ctx.enter_context(nc.sbuf_tensor(pfx + name, shape, dt))

        def ps(name, shape, dt):
            return ctx.enter_context(nc.psum_tensor(pfx + name, shape, dt))

        def sem(name):
            return osem(pfx + name)

        def dw(eng):
            for s, v in deps:
                eng.wait_ge(s, v)

        xstk = [sb(f"xstk{i}", [128, XW], BF16) for i in range(2)]
        wt = [sb(f"wt{i}", [128, 512], BF16) for i in range(2)]
        slo = [sb(f"slo{i}", [64, D], F32) for i in range(2)]
        idxs = sb("idxs", [64, max(nslot, 1)], I32)
        ident = sb("ident", [128, 128], F32)
        zt = [sb(f"zt{i}", [128, D], F32) for i in range(2)]
        zq = sb("zq", [128, D], F32)
        xln = [sb(f"xln{i}", [128, D], F32) for i in range(2)]
        fmsb = [sb(f"fmsb{i}", [128, D], F32) for i in range(2)]
        st = sb("st", [128, 12], F32)
        eps_t = sb("eps_c", [128, 1], F32)
        PAc = ps("PAc", [128, 2048], F32)
        PBc = ps("PBc", [128, 2048], F32)

        s_ini = sem("s_ini")
        xk = [sem("xk0"), sem("xk1")]
        cv_t = sem("cv_t")
        ev_a = sem("ev_a")
        sc_g = sem("sc_g")
        zld = sem("zld")
        st_v = sem("st_v")
        sq_a = sem("sq_a")
        ap_a = sem("ap_a")
        tr_t = sem("tr_t")
        sr_a = sem("sr_a")
        fm_v = sem("fm_v")
        ost = [sem("ost0"), sem("ost1")]
        ini_v = sem("ini_vc")
        ini_w = sem("ini_wc")

        NR = NTOK // 128
        V_PER_R = 9

        @block_ctx(ctx, nc)
        def _build(block):
            @block.sync
            def _(sq):
                dw(sq)
                sq.dma_start(out=zdr[:, :], in_=x0[:, :]).then_inc(s_ini, 16)
                sq.dma_start(out=idxs[:, 0:nslot], in_=idx[:, :]).then_inc(s_ini, 16)
                sq.dma_start(out=ident[:, :], in_=idf[:, :]).then_inc(s_ini, 16)
                for r in range(NR):
                    sq.wait_ge(sc_g, 16 * nslot)
                    if r >= 2:
                        sq.wait_ge(sq_a, r - 1)
                        sq.wait_ge(ap_a, r - 1)
                    sq.dma_start(out=zt[r % 2][:, :],
                                 in_=zdr[r * 128:(r + 1) * 128, :]).then_inc(zld, 16)
                    sq.wait_ge(fm_v, r + 1)
                    for c in range(NDC):
                        sq.dma_start(
                            out=x1fm[c * 128:(c + 1) * 128, r * 128:(r + 1) * 128],
                            in_=fmsb[r % 2][:, c * 128:(c + 1) * 128]).then_inc(ost[r % 2], 16)
                sq.wait_ge(ost[0], 16 * NDC * NR // 2)
                sq.wait_ge(ost[1], 16 * NDC * NR // 2)

            @block.gpsimd
            def _(g):
                dw(g)
                g.memset(xstk[0][:, :], 0.0)
                g.memset(xstk[1][:, :], 0.0).then_inc(ini_v, 1)
                g.wait_ge(ini_v, 1)
                for s in range(nslot):
                    p = s % 2
                    if s >= 2:
                        g.wait_ge(cv_t, s - 1)
                    g.dma_start(out=xstk[p][0:64, 15:15 + D],
                                in_=xsrc[s]).then_inc(xk[p], 16)
                    g.dma_start(out=xstk[p][64:128, 7:7 + D],
                                in_=xsrc[s]).then_inc(xk[p], 16)
                    g.dma_start(out=wt[p][:, :], in_=wsl[s]).then_inc(xk[p], 16)
                    if s >= 1:
                        g.wait_ge(ev_a, s)
                        g.wait_ge(s_ini, 48)
                        g.indirect_dma_start(
                            out=zdr[:, :],
                            out_offset=IndirectOffsetOnAxis(idxs[:, s - 1:s], 0),
                            in_=slo[(s - 1) % 2][:, :],
                            in_offset=None,
                            compute_op=ALU.add).then_inc(sc_g, 16)
                        g.wait_ge(sc_g, 16 * s)
                if nslot >= 1:
                    g.wait_ge(ev_a, nslot)
                    g.wait_ge(s_ini, 48)
                    g.indirect_dma_start(
                        out=zdr[:, :],
                        out_offset=IndirectOffsetOnAxis(idxs[:, nslot - 1:nslot], 0),
                        in_=slo[(nslot - 1) % 2][:, :],
                        in_offset=None,
                        compute_op=ALU.add).then_inc(sc_g, 16)
                    g.wait_ge(sc_g, 16 * nslot)

            @block.tensor
            def _(T):
                dw(T)
                T.wait_ge(ini_v, 1)
                for s in range(nslot):
                    p = s % 2
                    T.wait_ge(xk[p], 48 * (s // 2 + 1))
                    if s >= 2:
                        T.wait_ge(ev_a, s - 1)
                    psl = (PAc, PBc)[p]
                    for mt in range(4):
                        for tap in range(8):
                            mm = T.matmul(
                                psl[0:64, mt * 512:(mt + 1) * 512],
                                wt[p][:, tap * 64:(tap + 1) * 64],
                                xstk[p][:, tap + mt * 512: tap + mt * 512 + 512],
                                start=(tap == 0), stop=(tap == 7))
                    mm.then_inc(cv_t, 1)
                T.wait_ge(s_ini, 48)
                for r in range(NR):
                    T.wait_ge(ap_a, r + 1)
                    if r >= 2:
                        T.wait_ge(fm_v, r - 1)
                    if r < 2:
                        T.wait_ge(ev_a, nslot)
                    pst = (PAc, PBc)[r % 2]
                    for c in range(NDC):
                        mm = T.transpose(pst[:, c * 128:(c + 1) * 128],
                                         xln[r % 2][:, c * 128:(c + 1) * 128],
                                         ident[:, :])
                    mm.then_inc(tr_t, 1)

            @block.scalar
            def _(A):
                dw(A)
                for s in range(nslot):
                    p = s % 2
                    A.wait_ge(cv_t, s + 1)
                    if s >= 2:
                        A.wait_ge(sc_g, 16 * (s - 1))
                    A.activation(out=slo[p][:, :], in_=(PAc, PBc)[p][0:64, :],
                                 func=AF.Identity).then_inc(ev_a, 1)
                A.wait_ge(ini_w, 1)
                for r in range(NR):
                    vb = V_PER_R * r
                    A.wait_ge(zld, 16 * (r + 1))
                    if r >= 1:
                        A.wait_ge(st_v, V_PER_R * (r - 1) + 2)
                    A.activation(out=zq[:, :], in_=zt[r % 2][:, :],
                                 func=AF.Square).then_inc(sq_a, 1)
                    A.wait_ge(st_v, vb + 6)
                    A.activation(out=st[:, 6:7], in_=st[:, 5:6], func=AF.Sqrt,
                                 bias=eps_t[:, 0:1]).then_inc(sr_a, 1)
                    A.wait_ge(st_v, vb + 9)
                    if r >= 2:
                        A.wait_ge(tr_t, r - 1)
                    A.activation(out=xln[r % 2][:, :], in_=zt[r % 2][:, :],
                                 func=AF.Identity, scale=st[:, 7:8],
                                 bias=st[:, 9:10]).then_inc(ap_a, 1)

            @block.vector
            def _(V):
                dw(V)
                V.memset(eps_t[:, :], EPS_LN).then_inc(ini_w, 1)
                for r in range(NR):
                    vb = V_PER_R * r
                    V.wait_ge(zld, 16 * (r + 1))
                    V.reduce_sum(out=st[:, 0:1], in_=zt[r % 2][:, :], axis=AX.X).then_inc(st_v, 1)
                    V.wait_ge(sq_a, r + 1)
                    V.reduce_sum(out=st[:, 1:2], in_=zq[:, :], axis=AX.X).then_inc(st_v, 1)
                    V.wait_ge(st_v, vb + 1)
                    V.tensor_scalar_mul(out=st[:, 2:3], in0=st[:, 0:1], scalar1=1.0 / D).then_inc(st_v, 1)
                    V.wait_ge(st_v, vb + 2)
                    V.tensor_scalar_mul(out=st[:, 3:4], in0=st[:, 1:2], scalar1=1.0 / D).then_inc(st_v, 1)
                    V.wait_ge(st_v, vb + 3)
                    V.tensor_tensor(out=st[:, 4:5], in0=st[:, 2:3], in1=st[:, 2:3], op=ALU.mult).then_inc(st_v, 1)
                    V.wait_ge(st_v, vb + 5)
                    V.tensor_tensor(out=st[:, 5:6], in0=st[:, 3:4], in1=st[:, 4:5], op=ALU.subtract).then_inc(st_v, 1)
                    V.wait_ge(sr_a, r + 1)
                    V.reciprocal(out=st[:, 7:8], in_=st[:, 6:7]).then_inc(st_v, 1)
                    V.wait_ge(st_v, vb + 7)
                    V.tensor_tensor(out=st[:, 8:9], in0=st[:, 2:3], in1=st[:, 7:8], op=ALU.mult).then_inc(st_v, 1)
                    V.wait_ge(st_v, vb + 8)
                    V.tensor_scalar_mul(out=st[:, 9:10], in0=st[:, 8:9], scalar1=-1.0).then_inc(st_v, 1)
                    V.wait_ge(tr_t, r + 1)
                    if r >= 2:
                        V.wait_ge(ost[r % 2], 16 * NDC * (r // 2))
                    for bank in range(4):
                        cp = V.tensor_copy(out=fmsb[r % 2][:, bank * 512:(bank + 1) * 512],
                                           in_=(PAc, PBc)[r % 2][:, bank * 512:(bank + 1) * 512])
                    cp.then_inc(fm_v, 1)

    return [(ost[0], 16 * NDC * NR // 2), (ost[1], 16 * NDC * NR // 2)]


def block_ctx(ctx, nc):
    block = ctx.enter_context(nc.Block())

    def deco(fn):
        fn(block)
        return fn
    return deco


def conv_plan(rows, cols):
    rows = np.asarray(rows)
    cnt = {}
    for j in range(len(rows)):
        cnt.setdefault(int(rows[j]), []).append(j)
    rv = sorted(cnt, key=lambda r: -len(cnt[r]))
    side = {0: [], 1: []}
    load = [0, 0]
    for r in rv:
        s = 0 if (load[0] <= load[1] and len(side[0]) < 32) or len(side[1]) >= 32 else 1
        side[s].append(r)
        load[s] += len(cnt[r])
    rest = [r for r in range(64) if r not in cnt]
    while len(side[0]) < 32:
        side[0].append(rest.pop())
    while len(side[1]) < 32:
        side[1].append(rest.pop())
    slots = {h: [(pos, j) for pos, r in enumerate(side[h]) for j in cnt.get(r, [])]
             for h in (0, 1)}
    nslot = max(len(slots[0]), len(slots[1]), 1)
    nslot = (nslot + 7) // 8 * 8
    return side, slots, nslot


def conv_stage_half(x0b, block_values, cols, side_rows, side_slots, nslot):
    cols = np.asarray(cols)
    bv = np.asarray(block_values)
    x0b = np.asarray(x0b)
    x0_own = np.ascontiguousarray(
        x0b.reshape(64, 64, D)[side_rows].reshape(NTOK, D)).astype(np.float32)
    xsrc = np.zeros((nslot, 64, D), np.float32)
    wsl = np.zeros((nslot, 128, 512), np.float32)
    idx = np.zeros((64, nslot), np.int32)
    for s, (pos, j) in enumerate(side_slots):
        c = int(cols[j])
        xsrc[s] = x0b[c * 64:(c + 1) * 64, :]
        arr = bv[j].transpose(1, 2, 0)
        wsl[s] = np.concatenate([arr[:, :8, :], arr[:, 8:, :]], axis=0).reshape(128, 512)
        idx[:, s] = pos * 64 + np.arange(64)
    return {"x0": x0_own, "xsrc": xsrc, "wsl": wsl, "idx": idx,
            "idf": np.eye(128, dtype=np.float32)}



def build_kv():
    nc = bass.Bass("TRN2", target_bir_lowering=False, debug=False)
    xfm = nc.dram_tensor("xfm", [D, NTOK], F32, kind="ExternalInput").ap()
    Wk = nc.dram_tensor("Wk", [D, 2 * DH * NHEAD], F32, kind="ExternalInput").ap()
    Wv = nc.dram_tensor("Wv", [D, D], F32, kind="ExternalInput").ap()
    kout = nc.dram_tensor("kout", [2 * DH * NHEAD, NTOK], BF16, kind="ExternalOutput").ap()
    vout = nc.dram_tensor("vout", [NTOK, D], BF16, kind="ExternalOutput").ap()
    with contextlib.ExitStack() as octx:
        def osem(name):
            return octx.enter_context(nc.semaphore(name))
        emit_kv(nc, osem, "K", xfm, Wk, Wv, kout, vout, [])
    return nc


def emit_kv(nc, osem, pfx, xfm, Wk, Wv, kout, vout, deps):
    NKT = 2 * DH * NHEAD // 128

    with contextlib.ExitStack() as ctx:
        def sb(name, shape, dt):
            return ctx.enter_context(nc.sbuf_tensor(pfx + name, shape, dt))

        def ps(name, shape, dt):
            return ctx.enter_context(nc.psum_tensor(pfx + name, shape, dt))

        def sem(name):
            return osem(pfx + name)

        def dw(eng):
            for s, v in deps:
                eng.wait_ge(s, v)

        xbf = sb("xbf", [128, NDC * NTOK], BF16)
        wvbf = sb("wvbf", [128, NDC * D], BF16)
        wk_t = [sb(f"wk{i}", [128, NDC * 128], BF16) for i in range(2)]
        kbf = [sb(f"kbf{i}", [128, NTOK], BF16) for i in range(2)]
        vbf = [sb(f"vbf{i}", [128, D], BF16) for i in range(2)]
        PS0 = ps("PS0", [128, 2048], F32)
        PS1 = ps("PS1", [128, 2048], F32)

        x_ld = sem("x_ld")
        wv_ld = sem("wv_ld")
        wk_p = [sem("wk0"), sem("wk1")]
        mm_t = sem("mm_t")
        ev_a = sem("ev_a")
        ost = [sem("ostk0"), sem("ostk1")]

        @block_ctx(ctx, nc)
        def _build(block):
            @block.gpsimd
            def _(g):
                dw(g)
                for dc in range(NDC):
                    g.dma_start(out=xbf[:, dc * NTOK:(dc + 1) * NTOK],
                                in_=xfm[dc * 128:(dc + 1) * 128, :]).then_inc(x_ld, 16)
                for ht in range(NKT):
                    if ht >= 2:
                        g.wait_ge(mm_t, ht - 1)
                    g.dma_start(
                        out=wk_t[ht % 2][:, :],
                        in_=Wk[:, ht * 128:(ht + 1) * 128].rearrange(
                            "(dc p) h -> p dc h", p=128)).then_inc(wk_p[ht % 2], 16)
                g.wait_ge(mm_t, NKT - 1)
                for dc in range(NDC):
                    g.dma_start(out=wvbf[:, dc * D:(dc + 1) * D],
                                in_=Wv[dc * 128:(dc + 1) * 128, :]).then_inc(wv_ld, 16)

            @block.tensor
            def _(T):
                dw(T)
                T.wait_ge(x_ld, 16 * NDC)
                for ht in range(NKT):
                    psK = (PS0, PS1)[ht % 2]
                    T.wait_ge(wk_p[ht % 2], 16 * (ht // 2 + 1))
                    if ht >= 2:
                        T.wait_ge(ev_a, ht - 1)
                    for tt in range(4):
                        for dc in range(NDC):
                            mm = T.matmul(psK[:, tt * 512:(tt + 1) * 512],
                                          wk_t[ht % 2][:, dc * 128:(dc + 1) * 128],
                                          xbf[:, dc * NTOK + tt * 512: dc * NTOK + tt * 512 + 512],
                                          start=(dc == 0), stop=(dc == NDC - 1))
                    mm.then_inc(mm_t, 1)
                T.wait_ge(wv_ld, 16 * NDC)
                for tq in range(NDC):
                    it = NKT + tq
                    psV = (PS0, PS1)[tq % 2]
                    if it >= 2:
                        T.wait_ge(ev_a, it - 1)
                    for hv in range(4):
                        for dc in range(NDC):
                            mm = T.matmul(psV[:, hv * 512:(hv + 1) * 512],
                                          xbf[:, dc * NTOK + tq * 128: dc * NTOK + tq * 128 + 128],
                                          wvbf[:, dc * D + hv * 512: dc * D + hv * 512 + 512],
                                          start=(dc == 0), stop=(dc == NDC - 1))
                    mm.then_inc(mm_t, 1)

            @block.scalar
            def _(A):
                dw(A)
                for it in range(NKT + NDC):
                    p = it % 2
                    A.wait_ge(mm_t, it + 1)
                    if it >= 2:
                        A.wait_ge(ost[p], 16 * (it // 2))
                    dst = kbf[p] if it < NKT else vbf[p]
                    A.activation(out=dst[:, :], in_=(PS0, PS1)[p][:, :],
                                 func=AF.Identity).then_inc(ev_a, 1)

            @block.sync
            def _(sq):
                dw(sq)
                for it in range(NKT + NDC):
                    p = it % 2
                    sq.wait_ge(ev_a, it + 1)
                    if it < NKT:
                        sq.dma_start(out=kout[it * 128:(it + 1) * 128, :],
                                     in_=kbf[p][:, :]).then_inc(ost[p], 16)
                    else:
                        tq = it - NKT
                        sq.dma_start(out=vout[tq * 128:(tq + 1) * 128, :],
                                     in_=vbf[p][:, :]).then_inc(ost[p], 16)
                sq.wait_ge(ost[0], 16 * ((NKT + NDC + 1) // 2))
                sq.wait_ge(ost[1], 16 * ((NKT + NDC) // 2))
    return [(ost[0], 16 * ((NKT + NDC + 1) // 2)), (ost[1], 16 * ((NKT + NDC) // 2))]



def build_attn():
    nc = bass.Bass("TRN2", target_bir_lowering=False, debug=False)
    xfm = nc.dram_tensor("xfm", [D, NTOK], F32, kind="ExternalInput").ap()
    kfull = nc.dram_tensor("kfull", [2 * DH * NHEAD, SFULL], BF16, kind="ExternalInput").ap()
    vfull = nc.dram_tensor("vfull", [SFULL, D], BF16, kind="ExternalInput").ap()
    Wq = nc.dram_tensor("Wq", [D, 2 * DH * NHEAD], F32, kind="ExternalInput").ap()
    Wo = nc.dram_tensor("Wo", [D, D], F32, kind="ExternalInput").ap()
    nsT = nc.dram_tensor("nsT", [128, NDC], F32, kind="ExternalInput").ap()
    lam = nc.dram_tensor("lam", [1, NHEAD], F32, kind="ExternalInput").ap()
    idf = nc.dram_tensor("idf", [128, 128], F32, kind="ExternalInput").ap()
    x3fm = nc.dram_tensor("x3fm", [D, NTOK], F32, kind="ExternalOutput").ap()
    oscr = nc.dram_tensor("oscr", [D, NTOK], BF16).ap()
    zscr = nc.dram_tensor("zscr3", [D, NTOK], F32).ap()
    kv_srcs = ([kfull[:, 0:NTOK], kfull[:, NTOK:SFULL]],
               [vfull[0:NTOK, :], vfull[NTOK:SFULL, :]])
    with contextlib.ExitStack() as octx:
        def osem(name):
            return octx.enter_context(nc.semaphore(name))
        emit_attn(nc, osem, "A", xfm, kv_srcs, Wq, Wo, nsT, lam, idf, x3fm,
                  oscr, zscr, [])
    return nc


def emit_attn(nc, osem, pfx, xfm, kv_srcs, Wq, Wo, nsT, lam, idf, x3fm,
              oscr, zscr, deps):
    khalf, vhalf = kv_srcs
    NST = 16
    ISC = 1.0 / float(np.sqrt(DH))

    TB = 1
    AB = 1
    VB = 3

    def T_q(h, half):
        return TB + 198 * h + half + 1

    def T_s(h, st, se, q):
        return TB + 198 * h + 2 + (st // 4) * 49 + (st % 4) * 12 + (se - 1) * 4 + q + 1

    def T_tr(h, st, g):
        return TB + 198 * h + 2 + (st // 4) * 49 + (st % 4) * 12 + 8 + g + 1

    def T_av(h, sb):
        return TB + 198 * h + 2 + sb * 49 + 49

    def A_q(h):
        return AB + 198 * h + 2

    def A_e1(h, st, q):
        return AB + 198 * h + 2 + (st // 4) * 49 + (st % 4) * 12 + q + 1

    def A_e2(h, st, q):
        return A_e1(h, st, q) + 4

    def A_e3(h, st, g):
        return AB + 198 * h + 2 + (st // 4) * 49 + (st % 4) * 12 + 8 + g + 1

    def A_av(h, sb):
        return AB + 198 * h + 2 + sb * 49 + 49

    def V_i(h, st, k):
        return VB + 176 * h + 11 * st + k

    T2 = TB + 198 * NHEAD
    A2 = AB + 198 * NHEAD
    V2 = VB + 176 * NHEAD

    with contextlib.ExitStack() as ctx:
        def sb(name, shape, dt):
            return ctx.enter_context(nc.sbuf_tensor(pfx + name, shape, dt))

        def ps(name, shape, dt):
            return ctx.enter_context(nc.psum_tensor(pfx + name, shape, dt))

        def sem(name):
            return osem(pfx + name)

        def dw(eng):
            for s, v in deps:
                eng.wait_ge(s, v)

        xbf = sb("xbf", [128, NDC * NTOK], BF16)
        k_sb = [sb(f"ksb{i}", [128, 2 * SFULL], BF16) for i in range(2)]
        v_sb = [sb(f"vsb{i}", [128, SFULL], BF16) for i in range(2)]
        q_sb = sb("q_sb", [128, 2 * NTOK], BF16)
        wq_t = [sb(f"wqt{i}", [128, NDC * 256], BF16) for i in range(2)]
        e1 = sb("e1", [128, SFULL], BF16)
        e2 = sb("e2", [128, SFULL], BF16)
        t1e = sb("t1e", [128, SFULL], BF16)
        dsb = sb("dsb", [128, SFULL], BF16)
        w3uT = sb("w3uT", [128, 32 * 512], BF16)
        pacc = sb("pacc", [128, 20], F32)
        lamb = sb("lamb", [128, NHEAD], F32)
        lambn = sb("lambn", [128, NHEAD], F32)
        lamr = sb("lamr", [1, NHEAD], F32)
        sft = sb("sft", [1, 2], F32)
        ns_s = sb("ns_s", [128, NDC], F32)
        ones_rf = sb("ones_rf", [1, 128], F32)
        ones_cf = sb("ones_cfa", [128, 1], F32)
        eps_r = sb("eps_r", [1, 1], F32)
        eps_l = sb("eps_l", [1, 1], F32)
        o_sb = [sb(f"osb{i}", [128, 512], BF16) for i in range(2)]
        idb = sb("idb", [128, 128], BF16)
        statA = k_sb[0][:, 0:4096].bitcast(F32)
        statB = k_sb[0][:, 4096:8192].bitcast(F32)
        statC = k_sb[1][:, 0:4096].bitcast(F32)
        Rsb = k_sb[1][:, 4096:8192].bitcast(F32)
        wo_t = [dsb[:, 0:2048], dsb[:, 2048:4096]]
        obuf = [e2[:, 0:2048], t1e[:, 0:2048]]
        oq = v_sb[0][:, :].bitcast(F32)
        xres = [v_sb[1][:, :].bitcast(F32), w3uT[:, 0:4096].bitcast(F32)]
        z3sb = [w3uT[:, 4096:8192].bitcast(F32), w3uT[:, 8192:12288].bitcast(F32)]
        PS = ps("PS", [128, 4096], F32)

        t_sc = sem("t_sc")
        x_ld = sem("x_lda")
        a_ex = sem("a_ex")
        v_cb = sem("v_cb")
        ini = sem("ini")
        wq_p = [sem("wqp0"), sem("wqp1")]
        kv_p = [sem("kvp0"), sem("kvp1")]
        ost_o = [sem("osto0"), sem("osto1")]
        o_ld = [sem("old0"), sem("old1")]
        o2_ld = [sem("o2ld0"), sem("o2ld1")]
        wo_p = [sem("wop0"), sem("wop1")]
        xr_ld = [sem("xrld0"), sem("xrld1")]
        z3st = [sem("z3st0"), sem("z3st1")]
        zld2 = [sem("zld20"), sem("zld21")]
        ostx = [sem("ostx0"), sem("ostx1")]
        ini_v = sem("ini_va")

        M1r = statA[0:1, :]
        M2r = statB[0:1, :]
        VARr = statC[0:1, :]
        SVARr = q_sb.ap().bitcast(F32)[0:1, 0:2048]
        RSTDr = wq_t[0].ap().bitcast(F32)[0:1, 0:2048]
        NMBr = wq_t[1].ap().bitcast(F32)[0:1, 0:2048]

        def w3v(g, stq):
            return w3uT.ap().rearrange("p (tc s) -> p tc s", s=512)[
                :, g * 8:(g + 1) * 8, stq * 128:(stq + 1) * 128]

        @block_ctx(ctx, nc)
        def _build(block):
            @block.sync
            def _(sq):
                dw(sq)
                sq.dma_start(out=lamr[:, :], in_=lam[:, :]).then_inc(ini, 16)
                sq.dma_start(out=ns_s[:, :], in_=nsT[:, :]).then_inc(ini, 16)
                for h in range(NHEAD):
                    p = h % 2
                    if h >= 1:
                        for s_b in range(4):
                            i = (h - 1) * 4 + s_b
                            sq.wait_ge(a_ex, A_av(h - 1, s_b))
                            sq.dma_start(
                                out=oscr[(h - 1) * 128:h * 128, s_b * 512:(s_b + 1) * 512],
                                in_=o_sb[i % 2][:, :]).then_inc(ost_o[i % 2], 16)
                    if h >= 2:
                        sq.wait_ge(t_sc, T_av(h - 1, 3))
                    for hv, ksrc in enumerate(khalf):
                        sq.dma_start(out=k_sb[p][:, hv * NTOK:(hv + 1) * NTOK],
                                     in_=ksrc[h * 256:h * 256 + 128, :]).then_inc(kv_p[p], 16)
                        sq.dma_start(out=k_sb[p][:, SFULL + hv * NTOK:SFULL + (hv + 1) * NTOK],
                                     in_=ksrc[h * 256 + 128:h * 256 + 256, :]).then_inc(kv_p[p], 16)
                        sq.dma_start(
                            out=v_sb[p][:, hv * NTOK:(hv + 1) * NTOK],
                            in_=vhalf[hv][:, h * 128:(h + 1) * 128].rearrange(
                                "(tc p) d -> p tc d", p=128)).then_inc(kv_p[p], 16)
                for s_b in range(4):
                    i = 15 * 4 + s_b
                    sq.wait_ge(a_ex, A_av(15, s_b))
                    sq.dma_start(out=oscr[15 * 128:16 * 128, s_b * 512:(s_b + 1) * 512],
                                 in_=o_sb[i % 2][:, :]).then_inc(ost_o[i % 2], 16)
                sq.wait_ge(ost_o[0], 16 * 32)
                sq.wait_ge(ost_o[1], 16 * 32)
                for c in range(NDC):
                    if c >= 2:
                        sq.wait_ge(a_ex, A2 + c - 1)
                    sq.dma_start(out=obuf[c % 2][:, :],
                                 in_=oscr[c * 128:(c + 1) * 128, :]).then_inc(o_ld[c % 2], 16)
                sq.wait_ge(a_ex, A2 + 16)
                for c in range(NDC):
                    if c >= 2:
                        sq.wait_ge(v_cb, V2 + c + 1)
                    sq.dma_start(out=obuf[c % 2][:, :],
                                 in_=oscr[c * 128:(c + 1) * 128, :]).then_inc(o2_ld[c % 2], 16)
                for dt in range(NDC):
                    if dt >= 2:
                        sq.wait_ge(v_cb, V2 + 18 + dt - 1)
                    sq.dma_start(out=xres[dt % 2][:, :],
                                 in_=xfm[dt * 128:(dt + 1) * 128, :]).then_inc(xr_ld[dt % 2], 16)
                    sq.wait_ge(v_cb, V2 + 18 + dt + 1)
                    sq.dma_start(out=zscr[dt * 128:(dt + 1) * 128, :],
                                 in_=z3sb[dt % 2][:, :]).then_inc(z3st[dt % 2], 16)
                for c in range(NDC):
                    if c == 0:
                        sq.wait_ge(z3st[0], 16 * 8)
                        sq.wait_ge(z3st[1], 16 * 8)
                    if c >= 2:
                        sq.wait_ge(a_ex, A2 + 18 + c - 1)
                        sq.wait_ge(t_sc, T2 + 32 + c)
                    sq.dma_start(out=z3sb[c % 2][:, :],
                                 in_=zscr[c * 128:(c + 1) * 128, :]).then_inc(zld2[c % 2], 16)
                for c in range(NDC):
                    if c >= 2:
                        sq.wait_ge(v_cb, V2 + 38 + 2 * (c - 1))
                    sq.dma_start(out=xres[c % 2][:, :],
                                 in_=zscr[c * 128:(c + 1) * 128, :]).then_inc(xr_ld[c % 2], 16)
                    sq.wait_ge(v_cb, V2 + 38 + 2 * (c + 1))
                    sq.dma_start(out=x3fm[c * 128:(c + 1) * 128, :],
                                 in_=z3sb[c % 2][:, :]).then_inc(ostx[c % 2], 16)
                sq.wait_ge(ostx[0], 16 * 8)
                sq.wait_ge(ostx[1], 16 * 8)

            @block.gpsimd
            def _(g):
                dw(g)
                g.dma_start(out=idb[:, :], in_=idf[:, :]).then_inc(x_ld, 16)
                for dc in range(NDC):
                    g.dma_start(out=xbf[:, dc * NTOK:(dc + 1) * NTOK],
                                in_=xfm[dc * 128:(dc + 1) * 128, :]).then_inc(x_ld, 16)
                for h in range(NHEAD):
                    if h >= 2:
                        g.wait_ge(t_sc, T_q(h - 2, 1))
                    g.dma_start(
                        out=wq_t[h % 2][:, :],
                        in_=Wq[:, h * 256:(h + 1) * 256].rearrange(
                            "(dc p) q -> p dc q", p=128)).then_inc(wq_p[h % 2], 16)
                for dt in range(NDC):
                    if dt == 0:
                        g.wait_ge(t_sc, T2)
                    if dt >= 2:
                        g.wait_ge(t_sc, T2 + 16 + dt)
                    g.dma_start(
                        out=wo_t[dt % 2][:, :],
                        in_=Wo[:, dt * 128:(dt + 1) * 128].rearrange(
                            "(dc p) o -> p dc o", p=128)).then_inc(wo_p[dt % 2], 16)

            @block.tensor
            def _(T):
                dw(T)
                T.wait_ge(ini, 32)
                T.wait_ge(ini_v, 1)
                T.wait_ge(x_ld, 16 * (NDC + 1))
                mm = T.matmul(PS[:, 0:16], ones_rf[:, :], lamr[:, :], start=True, stop=True)
                mm.then_inc(t_sc, 1)
                for h in range(NHEAD):
                    p = h % 2
                    T.wait_ge(wq_p[p], 16 * (h // 2 + 1))
                    if h >= 1:
                        T.wait_ge(a_ex, A_av(h - 1, 3))
                        T.wait_ge(a_ex, A_q(h - 1) + 0)
                    else:
                        T.wait_ge(a_ex, 1)
                    for half in range(2):
                        for tq in range(4):
                            for dc in range(NDC):
                                mm = T.matmul(
                                    PS[:, half * 2048 + tq * 512: half * 2048 + (tq + 1) * 512],
                                    wq_t[p][:, dc * 256 + half * 128: dc * 256 + (half + 1) * 128],
                                    xbf[:, dc * NTOK + tq * 512: dc * NTOK + (tq + 1) * 512],
                                    start=(dc == 0), stop=(dc == NDC - 1))
                        mm.then_inc(t_sc, 1)
                    T.wait_ge(kv_p[p], 96 * (h // 2 + 1))
                    for st in range(NST):
                        s_b, stq = st // 4, st % 4
                        for se in (1, 2):
                            for q in range(4):
                                if se == 1:
                                    if st == 0:
                                        T.wait_ge(a_ex, A_q(h))
                                    elif stq == 0 and q == 0 and s_b >= 1:
                                        T.wait_ge(a_ex, A_av(h, s_b - 1))
                                        T.wait_ge(a_ex, A_e3(h, st - 1, q))
                                    else:
                                        T.wait_ge(a_ex, A_e3(h, st - 1, q))
                                else:
                                    T.wait_ge(a_ex, A_e1(h, st, q))
                                lq = q_sb[:, (se - 1) * NTOK + s_b * 512 + stq * 128:
                                          (se - 1) * NTOK + s_b * 512 + (stq + 1) * 128]
                                for j in range(2):
                                    mm = T.matmul(
                                        PS[:, q * 1024 + j * 512: q * 1024 + (j + 1) * 512],
                                        lq,
                                        k_sb[p][:, (se - 1) * SFULL + q * 1024 + j * 512:
                                                (se - 1) * SFULL + q * 1024 + (j + 1) * 512],
                                        start=True, stop=True)
                                mm.then_inc(t_sc, 1)
                        T.wait_ge(v_cb, V_i(h, st, 11))
                        for g_ in range(4):
                            T.wait_ge(a_ex, A_e2(h, st, g_))
                            psT = PS[:, g_ * 1024:(g_ + 1) * 1024].bitcast(BF16)
                            for j in range(8):
                                tc = g_ * 8 + j
                                mm = T.transpose(psT[:, j * 128:(j + 1) * 128],
                                                 dsb[:, tc * 128:(tc + 1) * 128],
                                                 idb[:, :])
                            mm.then_inc(t_sc, 1)
                        if stq == 3:
                            T.wait_ge(a_ex, A_e3(h, st, 3))
                            for tc in range(32):
                                mm = T.matmul(PS[:, 0:512],
                                              v_sb[p][:, tc * 128:(tc + 1) * 128],
                                              w3uT[:, tc * 512:(tc + 1) * 512],
                                              start=(tc == 0), stop=(tc == 31))
                            mm.then_inc(t_sc, 1)
                for c in range(NDC):
                    T.wait_ge(a_ex, A2 + c + 1)
                    for tt in range(4):
                        mm = T.matmul(PS[0:1, tt * 1024: tt * 1024 + 512], ones_cf[:, :],
                                      oq[:, tt * 512:(tt + 1) * 512],
                                      start=(c == 0), stop=(c == NDC - 1))
                    mm.then_inc(t_sc, 1)
                T.wait_ge(a_ex, A2 + 17)
                T.wait_ge(v_cb, V2 + 2)
                for tt in range(4):
                    mm = T.matmul(PS[:, tt * 1024: tt * 1024 + 512], ones_rf[:, :],
                                  NMBr[:, tt * 512:(tt + 1) * 512], start=True, stop=True)
                mm.then_inc(t_sc, 1)
                T.wait_ge(a_ex, A2 + 18)
                for dt in range(NDC):
                    T.wait_ge(wo_p[dt % 2], 16 * (dt // 2 + 1))
                    T.wait_ge(v_cb, V2 + 2 + 16)
                    if dt >= 2:
                        T.wait_ge(v_cb, V2 + 18 + dt - 1)
                    psD = (PS[:, 0:2048], PS[:, 2048:4096])[dt % 2]
                    for tt in range(4):
                        for cc in range(NDC):
                            mm = T.matmul(psD[:, tt * 512:(tt + 1) * 512],
                                          wo_t[dt % 2][:, cc * 128:(cc + 1) * 128],
                                          xbf[:, cc * NTOK + tt * 512: cc * NTOK + (tt + 1) * 512],
                                          start=(cc == 0), stop=(cc == NDC - 1))
                    mm.then_inc(t_sc, 1)
                for c in range(NDC):
                    T.wait_ge(zld2[c % 2], 16 * (c // 2 + 1))
                    for tt in range(4):
                        mm = T.matmul(PS[0:1, tt * 1024: tt * 1024 + 512], ones_cf[:, :],
                                      z3sb[c % 2][:, tt * 512:(tt + 1) * 512],
                                      start=(c == 0), stop=(c == NDC - 1))
                    T.wait_ge(a_ex, A2 + 18 + c + 1)
                    for tt in range(4):
                        mm = T.matmul(PS[0:1, 512 + tt * 1024: 512 + tt * 1024 + 512],
                                      ones_cf[:, :],
                                      oq[:, tt * 512:(tt + 1) * 512],
                                      start=(c == 0), stop=(c == NDC - 1))
                    mm.then_inc(t_sc, 1)
                T.wait_ge(v_cb, V2 + 37)
                for tt in range(4):
                    mm = T.matmul(PS[:, tt * 1024: tt * 1024 + 512], ones_rf[:, :],
                                  RSTDr[:, tt * 512:(tt + 1) * 512], start=True, stop=True)
                mm.then_inc(t_sc, 1)
                T.wait_ge(v_cb, V2 + 38)
                for tt in range(4):
                    mm = T.matmul(PS[:, 512 + tt * 1024: 512 + tt * 1024 + 512], ones_rf[:, :],
                                  NMBr[:, tt * 512:(tt + 1) * 512], start=True, stop=True)
                mm.then_inc(t_sc, 1)

            @block.scalar
            def _(A):
                dw(A)
                A.wait_ge(t_sc, 1)
                A.activation(out=lamb[:, :], in_=PS[:, 0:16], func=AF.Identity).then_inc(a_ex, 1)
                for h in range(NHEAD):
                    if h >= 1:
                        A.wait_ge(t_sc, T_s(h - 1, 15, 2, 3))
                    A.wait_ge(t_sc, T_q(h, 0))
                    A.activation(out=q_sb[:, 0:NTOK], in_=PS[:, 0:2048],
                                 func=AF.Identity).then_inc(a_ex, 1)
                    A.wait_ge(t_sc, T_q(h, 1))
                    A.activation(out=q_sb[:, NTOK:2 * NTOK], in_=PS[:, 2048:4096],
                                 func=AF.Identity).then_inc(a_ex, 1)
                    for st in range(NST):
                        s_b, stq = st // 4, st % 4
                        A.wait_ge(v_cb, V_i(h, st, 0))
                        for q in range(4):
                            A.wait_ge(t_sc, T_s(h, st, 1, q))
                            A.activation(out=e1[:, q * 1024:(q + 1) * 1024],
                                         in_=PS[:, q * 1024:(q + 1) * 1024],
                                         func=AF.Exp, scale=ISC,
                                         accum_out=pacc[:, q:q + 1]).then_inc(a_ex, 1)
                        for q in range(4):
                            A.wait_ge(t_sc, T_s(h, st, 2, q))
                            A.activation(out=e2[:, q * 1024:(q + 1) * 1024],
                                         in_=PS[:, q * 1024:(q + 1) * 1024],
                                         func=AF.Exp, scale=ISC,
                                         accum_out=pacc[:, 4 + q:5 + q]).then_inc(a_ex, 1)
                        for g_ in range(4):
                            A.wait_ge(t_sc, T_tr(h, st, g_))
                            psT = PS[:, g_ * 1024:(g_ + 1) * 1024].bitcast(BF16)
                            A.activation(out=w3v(g_, stq), in_=psT[:, 0:1024],
                                         func=AF.Exp).then_inc(a_ex, 1)
                        if stq == 3:
                            i = h * 4 + s_b
                            A.wait_ge(t_sc, T_av(h, s_b))
                            if i >= 2:
                                A.wait_ge(ost_o[i % 2], 16 * (i // 2))
                            A.activation(out=o_sb[i % 2][:, :], in_=PS[:, 0:512],
                                         func=AF.Identity).then_inc(a_ex, 1)
                for c in range(NDC):
                    A.wait_ge(o_ld[c % 2], 16 * (c // 2 + 1))
                    if c >= 1:
                        A.wait_ge(t_sc, T2 + c)
                    A.activation(out=oq[:, :], in_=obuf[c % 2][:, :],
                                 func=AF.Square).then_inc(a_ex, 1)
                A.wait_ge(t_sc, T2 + 16)
                A.wait_ge(ini_v, 1)
                for tt in range(4):
                    aa = A.activation(out=M1r[:, tt * 512:(tt + 1) * 512],
                                      in_=PS[0:1, tt * 1024: tt * 1024 + 512],
                                      func=AF.Sqrt, scale=1.0 / D, bias=eps_r[0:1, 0:1])
                aa.then_inc(a_ex, 1)
                A.wait_ge(t_sc, T2 + 17)
                for tt in range(4):
                    aa = A.activation(out=Rsb[:, tt * 512:(tt + 1) * 512],
                                      in_=PS[:, tt * 1024: tt * 1024 + 512],
                                      func=AF.Identity)
                aa.then_inc(a_ex, 1)
                for c in range(NDC):
                    A.wait_ge(zld2[c % 2], 16 * (c // 2 + 1))
                    if c >= 1:
                        A.wait_ge(t_sc, T2 + 33 + c)
                    A.activation(out=oq[:, :], in_=z3sb[c % 2][:, :],
                                 func=AF.Square).then_inc(a_ex, 1)
                A.wait_ge(t_sc, T2 + 49)
                for tt in range(4):
                    aa = A.activation(out=M1r[:, tt * 512:(tt + 1) * 512],
                                      in_=PS[0:1, tt * 1024: tt * 1024 + 512],
                                      func=AF.Identity, scale=1.0 / D)
                for tt in range(4):
                    aa = A.activation(out=M2r[:, tt * 512:(tt + 1) * 512],
                                      in_=PS[0:1, 512 + tt * 1024: 512 + tt * 1024 + 512],
                                      func=AF.Identity, scale=1.0 / D)
                aa.then_inc(a_ex, 1)
                A.wait_ge(v_cb, V2 + 36)
                A.activation(out=SVARr, in_=VARr, func=AF.Sqrt,
                             bias=eps_l[0:1, 0:1]).then_inc(a_ex, 1)
                A.wait_ge(t_sc, T2 + 50)
                for tt in range(4):
                    aa = A.activation(out=Rsb[:, tt * 512:(tt + 1) * 512],
                                      in_=PS[:, tt * 1024: tt * 1024 + 512],
                                      func=AF.Identity)
                aa.then_inc(a_ex, 1)
                A.wait_ge(t_sc, T2 + 51)
                for tt in range(4):
                    aa = A.activation(out=oq[:, tt * 512:(tt + 1) * 512],
                                      in_=PS[:, 512 + tt * 1024: 512 + tt * 1024 + 512],
                                      func=AF.Identity)
                aa.then_inc(a_ex, 1)

            @block.vector
            def _(V):
                dw(V)
                V.memset(ones_rf[:, :], 1.0)
                V.memset(ones_cf[:, :], 1.0)
                V.memset(eps_r[:, :], EPS_RMS)
                V.memset(eps_l[:, :], EPS_LN).then_inc(ini_v, 1)
                V.wait_ge(a_ex, 1)
                V.tensor_scalar_mul(out=lambn[:, :], in0=lamb[:, :], scalar1=-1.0).then_inc(v_cb, 1)
                V.wait_ge(ini, 32)
                V.reduce_sum(out=sft[:, 0:1], in_=lamr[:, :], axis=AX.X).then_inc(v_cb, 1)
                V.wait_ge(v_cb, 2)
                V.tensor_scalar(out=sft[:, 1:2], in0=sft[:, 0:1], scalar1=-1.0 / NHEAD,
                                scalar2=1.0, op0=ALU.mult, op1=ALU.add).then_inc(v_cb, 1)
                for h in range(NHEAD):
                    for st in range(NST):
                        base = V_i(h, st, 0)
                        V.wait_ge(a_ex, A_e1(h, st, 3))
                        V.tensor_tensor(out=pacc[:, 8:9], in0=pacc[:, 0:1], in1=pacc[:, 1:2],
                                        op=ALU.add).then_inc(v_cb, 1)
                        V.tensor_tensor(out=pacc[:, 9:10], in0=pacc[:, 2:3], in1=pacc[:, 3:4],
                                        op=ALU.add).then_inc(v_cb, 1)
                        V.wait_ge(v_cb, base + 2)
                        V.tensor_tensor(out=pacc[:, 10:11], in0=pacc[:, 8:9], in1=pacc[:, 9:10],
                                        op=ALU.add).then_inc(v_cb, 1)
                        V.wait_ge(v_cb, base + 3)
                        V.reciprocal(out=pacc[:, 11:12], in_=pacc[:, 10:11]).then_inc(v_cb, 1)
                        V.wait_ge(a_ex, A_e2(h, st, 3))
                        V.tensor_tensor(out=pacc[:, 12:13], in0=pacc[:, 4:5], in1=pacc[:, 5:6],
                                        op=ALU.add).then_inc(v_cb, 1)
                        V.tensor_tensor(out=pacc[:, 13:14], in0=pacc[:, 6:7], in1=pacc[:, 7:8],
                                        op=ALU.add).then_inc(v_cb, 1)
                        V.wait_ge(v_cb, base + 6)
                        V.tensor_tensor(out=pacc[:, 14:15], in0=pacc[:, 12:13], in1=pacc[:, 13:14],
                                        op=ALU.add).then_inc(v_cb, 1)
                        V.wait_ge(v_cb, base + 7)
                        V.reciprocal(out=pacc[:, 15:16], in_=pacc[:, 14:15]).then_inc(v_cb, 1)
                        V.wait_ge(v_cb, base + 8)
                        V.tensor_tensor(out=pacc[:, 16:17], in0=pacc[:, 15:16],
                                        in1=lambn[:, h:h + 1], op=ALU.mult).then_inc(v_cb, 1)
                        V.wait_ge(v_cb, base + 9)
                        V.tensor_scalar_mul(out=t1e[:, :], in0=e1[:, :],
                                            scalar1=pacc[:, 11:12]).then_inc(v_cb, 1)
                        V.wait_ge(v_cb, base + 10)
                        if st >= 1:
                            V.wait_ge(t_sc, T_tr(h, st - 1, 3))
                        elif h >= 1:
                            V.wait_ge(t_sc, T_tr(h - 1, 15, 3))
                        V.scalar_tensor_tensor(out=dsb[:, :], in0=e2[:, :],
                                               scalar=pacc[:, 16:17], in1=t1e[:, :],
                                               op0=ALU.mult, op1=ALU.add).then_inc(v_cb, 1)
                V.wait_ge(a_ex, A2 + 17)
                V.reciprocal(out=RSTDr, in_=M1r).then_inc(v_cb, 1)
                V.wait_ge(v_cb, V2 + 1)
                V.tensor_scalar(out=NMBr, in0=RSTDr, scalar1=sft[0:1, 1:2], scalar2=None,
                                op0=ALU.mult).then_inc(v_cb, 1)
                for c in range(NDC):
                    V.wait_ge(o2_ld[c % 2], 16 * (c // 2 + 1))
                    V.wait_ge(a_ex, A2 + 18)
                    V.scalar_tensor_tensor(out=xbf[:, c * NTOK:(c + 1) * NTOK],
                                           in0=obuf[c % 2][:, :], scalar=ns_s[:, c:c + 1],
                                           in1=Rsb[:, :], op0=ALU.mult,
                                           op1=ALU.mult).then_inc(v_cb, 1)
                for dt in range(NDC):
                    V.wait_ge(t_sc, T2 + 18 + dt)
                    V.wait_ge(xr_ld[dt % 2], 16 * (dt // 2 + 1))
                    if dt >= 2:
                        V.wait_ge(z3st[dt % 2], 16 * (dt // 2))
                    psD = (PS[:, 0:2048], PS[:, 2048:4096])[dt % 2]
                    V.tensor_tensor(out=z3sb[dt % 2][:, :], in0=psD[:, :],
                                    in1=xres[dt % 2][:, :], op=ALU.add).then_inc(v_cb, 1)
                V.wait_ge(a_ex, A2 + 35)
                V.tensor_tensor(out=VARr, in0=M1r, in1=M1r, op=ALU.mult).then_inc(v_cb, 1)
                V.wait_ge(v_cb, V2 + 35)
                V.tensor_tensor(out=VARr, in0=M2r, in1=VARr, op=ALU.subtract).then_inc(v_cb, 1)
                V.wait_ge(a_ex, A2 + 36)
                V.reciprocal(out=RSTDr, in_=SVARr).then_inc(v_cb, 1)
                V.wait_ge(v_cb, V2 + 37)
                V.tensor_tensor(out=NMBr, in0=M1r, in1=RSTDr, op=ALU.mult).then_inc(v_cb, 1)
                e1f = e1.ap().bitcast(F32)
                for c in range(NDC):
                    V.wait_ge(xr_ld[c % 2], 16 * (NDC // 2 + c // 2 + 1))
                    V.wait_ge(a_ex, A2 + 38)
                    if c >= 2:
                        V.wait_ge(ostx[c % 2], 16 * (c // 2))
                    V.tensor_tensor(out=e1f[:, :], in0=xres[c % 2][:, :], in1=Rsb[:, :],
                                    op=ALU.mult).then_inc(v_cb, 1)
                    V.wait_ge(v_cb, V2 + 38 + 2 * c + 1)
                    V.tensor_tensor(out=z3sb[c % 2][:, :], in0=e1f[:, :],
                                    in1=oq[:, :], op=ALU.subtract).then_inc(v_cb, 1)
    return [(ostx[0], 16 * 8), (ostx[1], 16 * 8)]



def kernel_unfused(**inputs):
    x0 = np.ascontiguousarray(np.asarray(inputs["L0_attn_x"]), dtype=np.float32)
    bv = np.asarray(inputs["L0_attn_block_values"])
    cols = np.asarray(inputs["L0_attn_cols"])
    rows = np.asarray(inputs["L0_attn_rows"])
    ident = np.eye(128, dtype=np.float32)

    side, slots, nslot = conv_plan(rows, cols)
    row_idx = {h: np.repeat(np.asarray(side[h], np.int64) * 64, 64)
               + np.tile(np.arange(64, dtype=np.int64), 32) for h in (0, 1)}

    nc_conv = build_conv(nslot)
    im1 = []
    for c in range(NCORE):
        b, h = c // 2, c % 2
        im1.append(conv_stage_half(x0[b], bv, cols, side[h], slots[h], nslot))
    r1 = run_bass_kernel_spmd(nc_conv, im1, core_ids=list(range(NCORE)))
    x1 = [r1.results[c]["x1fm"] for c in range(NCORE)]

    nc_ffn = build_ffn()
    im2 = ffn_in_maps(x1, inputs["L0_ffn_wq_gate"], inputs["L0_ffn_s_gate"],
                      inputs["L0_ffn_wq_up"], inputs["L0_ffn_s_up"],
                      inputs["L0_ffn_wq_down"], inputs["L0_ffn_s_down"])
    r2 = run_bass_kernel_spmd(nc_ffn, im2, core_ids=list(range(NCORE)))
    x2 = [r2.results[c]["yfm"] for c in range(NCORE)]

    nc_kv = build_kv()
    Wk = np.ascontiguousarray(np.asarray(inputs["L1_attn_Wk"], np.float32))
    Wv = np.ascontiguousarray(np.asarray(inputs["L1_attn_Wv"], np.float32))
    im3 = [{"xfm": x2[c], "Wk": Wk, "Wv": Wv} for c in range(NCORE)]
    r3 = run_bass_kernel_spmd(nc_kv, im3, core_ids=list(range(NCORE)))

    im4 = []
    Wq = np.ascontiguousarray(np.asarray(inputs["L1_attn_Wq"], np.float32))
    Wo = np.ascontiguousarray(np.asarray(inputs["L1_attn_Wo"], np.float32))
    nsT = np.ascontiguousarray(np.asarray(inputs["L1_attn_norm_scale"],
                                          np.float32).reshape(NDC, 128).T)
    lamr = np.asarray(inputs["L1_attn_lam"], np.float32).reshape(1, NHEAD)
    for c in range(NCORE):
        b = c // 2
        kf = np.concatenate([r3.results[2 * b]["kout"], r3.results[2 * b + 1]["kout"]], axis=1)
        vf = np.concatenate([r3.results[2 * b]["vout"], r3.results[2 * b + 1]["vout"]], axis=0)
        im4.append({"xfm": x2[c], "kfull": np.ascontiguousarray(kf),
                    "vfull": np.ascontiguousarray(vf), "Wq": Wq, "Wo": Wo,
                    "nsT": nsT, "lam": lamr, "idf": ident})

    nc_at = build_attn()
    r4 = run_bass_kernel_spmd(nc_at, im4, core_ids=list(range(NCORE)))
    x3 = [r4.results[c]["x3fm"] for c in range(NCORE)]

    im5 = ffn_in_maps(x3, inputs["L1_ffn_wq_gate"], inputs["L1_ffn_s_gate"],
                      inputs["L1_ffn_wq_up"], inputs["L1_ffn_s_up"],
                      inputs["L1_ffn_wq_down"], inputs["L1_ffn_s_down"])
    r5 = run_bass_kernel_spmd(nc_ffn, im5, core_ids=list(range(NCORE)))

    out = np.empty((4, 2 * NTOK, D), np.float32)
    for c in range(NCORE):
        b, h = c // 2, c % 2
        out[b, row_idx[h], :] = r5.results[c]["yfm"].T
    return out



def emit_xchg(nc, osem, kloc, vloc, sh_k, sh_v, sh_flag, tok, deps):
    xch = osem("Xdone")
    with contextlib.ExitStack() as ctx:
        tt = ctx.enter_context(nc.sbuf_tensor("Xtok", [1, 1], I32))
        ft = ctx.enter_context(nc.sbuf_tensor("Xflag", [1, 1], I32))
        ld = osem("Xld")
        ps_ = osem("Xps")
        block = ctx.enter_context(nc.Block())

        @block.gpsimd
        def _(g):
            for s, v in deps:
                g.wait_ge(s, v)
            g.dma_start(out=tt[:, :], in_=tok[:, :]).then_inc(ld, 16)
            g.wait_ge(ld, 16)
            pid = nc.partition_id(engines=[mybir.EngineType.Pool])
            with g.register("Xpar") as par:
                g.reg_alu(par, pid, 1, ALU.bitwise_and)
                with g.If_eq(par, 0):
                    g.dma_start(out=sh_k[0], in_=kloc[:, :]).then_inc(ld, 16)
                    g.dma_start(out=sh_v[0], in_=vloc[:, :]).then_inc(ld, 16)
                    g.wait_ge(ld, 48)
                    g.dma_start(out=sh_flag[0:1, :], in_=tt[:, :]).then_inc(ld, 16)
                with g.Else():
                    g.dma_start(out=sh_k[1], in_=kloc[:, :]).then_inc(ld, 16)
                    g.dma_start(out=sh_v[1], in_=vloc[:, :]).then_inc(ld, 16)
                    g.wait_ge(ld, 48)
                    g.dma_start(out=sh_flag[1:2, :], in_=tt[:, :]).then_inc(ld, 16)
                g.wait_ge(ld, 64)
                with (
                    g.register("Xtgt") as tgt,
                    g.register("Xexp") as expreg,
                    g.register("Xobs") as obs,
                    g.register("Xneq") as neq,
                ):
                    g.reg_load(expreg, tt[0:1, 0:1])
                    g.reg_mov(tgt, 16)

                    def poll():
                        with g.If_eq(par, 0):
                            g.dma_start(out=ft[:, :], in_=sh_flag[1:2, :]).then_inc(ps_, 16)
                        with g.Else():
                            g.dma_start(out=ft[:, :], in_=sh_flag[0:1, :]).then_inc(ps_, 16)
                        g.wait_ge(ps_, tgt)
                        g.reg_add(tgt, tgt, 16)
                        g.reg_load(obs, ft[0:1, 0:1])
                        g.reg_alu(neq, obs, expreg, ALU.not_equal)

                    poll()
                    with g.While(neq):
                        poll()
            g.engine_nop().then_inc(xch, 1)
    return [(xch, 1)]



def build_fused(nslot, upto="full"):
    nc = bass.Bass("TRN2", target_bir_lowering=False, debug=False)
    ap = lambda name, shape, dt: nc.dram_tensor(name, shape, dt, kind="ExternalInput").ap()
    x0 = ap("x0", [NTOK, D], F32)
    xsrc = ap("xsrc", [nslot, 64, D], F32)
    wsl = ap("wsl", [nslot, 128, 512], F32)
    idx = ap("idx", [64, nslot], I32)
    idf = ap("idf", [128, 128], F32)
    wgT0 = ap("wgT0", [D, HID], I32)
    wuT0 = ap("wuT0", [D, HID], I32)
    wdT0 = ap("wdT0", [HID, D], I32)
    sgT0 = ap("sgT0", [128, NHC], F32)
    suT0 = ap("suT0", [128, NHC], F32)
    sdT0 = ap("sdT0", [128, NDC], F32)
    wgT1 = ap("wgT1", [D, HID], I32)
    wuT1 = ap("wuT1", [D, HID], I32)
    wdT1 = ap("wdT1", [HID, D], I32)
    sgT1 = ap("sgT1", [128, NHC], F32)
    suT1 = ap("suT1", [128, NHC], F32)
    sdT1 = ap("sdT1", [128, NDC], F32)
    Wq = ap("Wq", [D, 2 * DH * NHEAD], F32)
    Wk = ap("Wk", [D, 2 * DH * NHEAD], F32)
    Wv = ap("Wv", [D, D], F32)
    Wo = ap("Wo", [D, D], F32)
    nsT = ap("nsT", [128, NDC], F32)
    lam = ap("lam", [1, NHEAD], F32)
    tok = ap("tok", [1, 1], I32)
    yfm = nc.dram_tensor("yfm", [D, NTOK], F32, kind="ExternalOutput").ap()

    zdr = nc.dram_tensor("zdr", [NTOK, D], F32).ap()
    x1 = nc.dram_tensor("x1", [D, NTOK], F32).ap()
    x2 = nc.dram_tensor("x2", [D, NTOK], F32).ap()
    x3 = nc.dram_tensor("x3", [D, NTOK], F32).ap()
    zscr0 = nc.dram_tensor("zscr0", [D, NTOK], F32).ap()
    zscr1 = nc.dram_tensor("zscr1", [D, NTOK], F32).ap()
    kloc = nc.dram_tensor("kloc", [2 * DH * NHEAD, NTOK], BF16).ap()
    vloc = nc.dram_tensor("vloc", [NTOK, D], BF16).ap()
    oscr = nc.dram_tensor("oscr", [D, NTOK], BF16).ap()
    sh_kt = nc.dram_tensor("sh_k", [2, 2 * DH * NHEAD, NTOK], BF16, addr_space="Shared").ap()
    sh_vt = nc.dram_tensor("sh_v", [2, NTOK, D], BF16, addr_space="Shared").ap()
    sh_flag = nc.dram_tensor("sh_fl", [2, 1], I32, addr_space="Shared").ap()

    with contextlib.ExitStack() as octx:
        def osem(name):
            return octx.enter_context(nc.semaphore(name))

        if upto == "conv":
            emit_conv(nc, osem, "C", nslot, x0, xsrc, wsl, idx, idf, yfm, zdr, [])
            return nc
        ev = emit_conv(nc, osem, "C", nslot, x0, xsrc, wsl, idx, idf, x1, zdr, [])
        if upto == "f0":
            emit_ffn(nc, osem, "F0", x1, wgT0, wuT0, wdT0, sgT0, suT0,
                     sdT0, yfm, zscr0, ev)
            return nc
        ev, fbank = emit_ffn(nc, osem, "F0", x1, wgT0, wuT0, wdT0, sgT0, suT0,
                             sdT0, x2, zscr0, ev)
        ev_kv = emit_kv(nc, osem, "K", x2, Wk, Wv, kloc, vloc, ev)
        ev_x = emit_xchg(nc, osem, kloc, vloc,
                         [sh_kt[0], sh_kt[1]], [sh_vt[0], sh_vt[1]],
                         sh_flag, tok, ev_kv)
        if upto == "xchg":
            with contextlib.ExitStack() as c2:
                dn = octx.enter_context(nc.semaphore("XOUT"))
                blk = c2.enter_context(nc.Block())

                @blk.gpsimd
                def _(g):
                    for s, v in ev_x:
                        g.wait_ge(s, v)
                    g.dma_start(out=yfm[0:D, 0:NTOK // 2],
                                in_=sh_kt[0][0:D, 0:NTOK // 2]).then_inc(dn, 16)
                    g.dma_start(out=yfm[0:D, NTOK // 2:NTOK],
                                in_=sh_kt[1][0:D, 0:NTOK // 2]).then_inc(dn, 16)
                    g.wait_ge(dn, 32)
            return nc
        kv_srcs = ([sh_kt[0], sh_kt[1]], [sh_vt[0], sh_vt[1]])
        ev = emit_attn(nc, osem, "A", x2, kv_srcs, Wq, Wo, nsT, lam, idf,
                       x3, oscr, zscr1, ev + ev_x)
        if upto == "attn":
            with contextlib.ExitStack() as c2:
                dn = octx.enter_context(nc.semaphore("XOUT"))
                blk = c2.enter_context(nc.Block())

                @blk.gpsimd
                def _(g):
                    for s, v in ev:
                        g.wait_ge(s, v)
                    g.dma_start(out=yfm[:, :], in_=x3[:, :]).then_inc(dn, 16)
                    g.wait_ge(dn, 16)
            return nc
        emit_ffn(nc, osem, "F1", x3, wgT1, wuT1, wdT1, sgT1, suT1, sdT1,
                 yfm, zscr1, ev, sem_bank=fbank)
    return nc


def _ffn_w(inputs, layer):
    wgT = np.ascontiguousarray(np.asarray(inputs[f"L{layer}_ffn_wq_gate"]).T)
    wuT = np.ascontiguousarray(np.asarray(inputs[f"L{layer}_ffn_wq_up"]).T)
    wdT = np.ascontiguousarray(np.asarray(inputs[f"L{layer}_ffn_wq_down"]).T)
    sgT = np.ascontiguousarray(np.asarray(inputs[f"L{layer}_ffn_s_gate"], np.float32).reshape(NHC, 128).T)
    suT = np.ascontiguousarray(np.asarray(inputs[f"L{layer}_ffn_s_up"], np.float32).reshape(NHC, 128).T)
    sdT = np.ascontiguousarray(np.asarray(inputs[f"L{layer}_ffn_s_down"], np.float32).reshape(NDC, 128).T)
    return wgT, wuT, wdT, sgT, suT, sdT


def kernel_fused(upto="full", **inputs):
    x0 = np.ascontiguousarray(np.asarray(inputs["L0_attn_x"]), dtype=np.float32)
    bv = np.asarray(inputs["L0_attn_block_values"])
    cols = np.asarray(inputs["L0_attn_cols"])
    rows = np.asarray(inputs["L0_attn_rows"])

    side, slots, nslot = conv_plan(rows, cols)
    row_idx = {h: np.repeat(np.asarray(side[h], np.int64) * 64, 64)
               + np.tile(np.arange(64, dtype=np.int64), 32) for h in (0, 1)}

    w0 = _ffn_w(inputs, 0)
    w1 = _ffn_w(inputs, 1)
    Wq = np.ascontiguousarray(np.asarray(inputs["L1_attn_Wq"], np.float32))
    Wk = np.ascontiguousarray(np.asarray(inputs["L1_attn_Wk"], np.float32))
    Wv = np.ascontiguousarray(np.asarray(inputs["L1_attn_Wv"], np.float32))
    Wo = np.ascontiguousarray(np.asarray(inputs["L1_attn_Wo"], np.float32))
    nsT = np.ascontiguousarray(np.asarray(inputs["L1_attn_norm_scale"],
                                          np.float32).reshape(NDC, 128).T)
    lamr = np.asarray(inputs["L1_attn_lam"], np.float32).reshape(1, NHEAD)
    tokv = np.array([[np.random.default_rng().integers(1, 2**31 - 1)]], np.int32)

    nc = build_fused(nslot, upto)
    im = []
    for c in range(NCORE):
        b, h = c // 2, c % 2
        m = conv_stage_half(x0[b], bv, cols, side[h], slots[h], nslot)
        m.update({
            "wgT0": w0[0], "wuT0": w0[1], "wdT0": w0[2],
            "sgT0": w0[3], "suT0": w0[4], "sdT0": w0[5],
            "wgT1": w1[0], "wuT1": w1[1], "wdT1": w1[2],
            "sgT1": w1[3], "suT1": w1[4], "sdT1": w1[5],
            "Wq": Wq, "Wk": Wk, "Wv": Wv, "Wo": Wo,
            "nsT": nsT, "lam": lamr, "tok": tokv,
        })
        im.append(m)
    res = run_bass_kernel_spmd(nc, im, core_ids=list(range(NCORE)))
    out = np.empty((4, 2 * NTOK, D), np.float32)
    for c in range(NCORE):
        b, h = c // 2, c % 2
        out[b, row_idx[h], :] = res.results[c]["yfm"].T
    return out


def kernel(**inputs):
    return kernel_fused(**inputs)


# version 05 (final)
# speedup vs baseline: 1.4094x; 1.4094x over previous
import contextlib
import numpy as np
import ml_dtypes
import concourse.bass as bass
import concourse.mybir as mybir
from concourse.bass import IndirectOffsetOnAxis
from concourse.bass_utils import run_bass_kernel_spmd

F32 = mybir.dt.float32
BF16 = mybir.dt.bfloat16
I32 = mybir.dt.int32
AF = mybir.ActivationFunctionType
ALU = mybir.AluOpType
AX = mybir.AxisListType

D = 2048
HID = 5632
NTOK = 2048
NCORE = 8
NHC = HID // 128
NDC = D // 128
TH = 1024
EPS_LN = 1e-5
EPS_RMS = 1e-6
NHEAD = 16
DH = 128
SFULL = 2 * NTOK

BF = ml_dtypes.bfloat16



def build_ffn():
    nc = bass.Bass("TRN2", target_bir_lowering=False, debug=False)
    xfm = nc.dram_tensor("xfm", [D, NTOK], F32, kind="ExternalInput").ap()
    wgT = nc.dram_tensor("wgT", [D, HID], I32, kind="ExternalInput").ap()
    wuT = nc.dram_tensor("wuT", [D, HID], I32, kind="ExternalInput").ap()
    wdT = nc.dram_tensor("wdT", [HID, D], I32, kind="ExternalInput").ap()
    sgT = nc.dram_tensor("sgT", [128, NHC], F32, kind="ExternalInput").ap()
    suT = nc.dram_tensor("suT", [128, NHC], F32, kind="ExternalInput").ap()
    sdT = nc.dram_tensor("sdT", [128, NDC], F32, kind="ExternalInput").ap()
    yfm = nc.dram_tensor("yfm", [D, NTOK], F32, kind="ExternalOutput").ap()
    zscr = nc.dram_tensor("zscr", [D, NTOK], F32).ap()
    with contextlib.ExitStack() as octx:
        def osem(name):
            return octx.enter_context(nc.semaphore(name))
        emit_ffn(nc, osem, "F", xfm, wgT, wuT, wdT, sgT, suT, sdT, yfm, zscr, [])
    return nc


FFN_FINAL = {
    "x_ld": 512, "sc_ld": 48, "wg0": 1408, "wg1": 1408, "wd0": 256, "wd1": 256,
    "gu_t": 88, "v_t1": 88, "gu_v": 88, "silu_a": 88, "a_v": 88, "dn_t": 32,
    "dn_v": 32, "z_v": 32, "q_a": 32, "st_t": 32, "xr_ld": 512, "zst": 512,
    "zl_ld": 512, "yst": 512, "t_ph": 12, "a_ph": 12, "ln_a": 2, "ln_v1": 6,
    "ln_a2": 2, "ln_v2": 2, "ap_v": 64, "ini_v": 1,
}


class _OffEng:

    def __init__(self, e, bases):
        self._e = e
        self._b = bases

    def wait_ge(self, s, v):
        return self._e.wait_ge(s, self._b.get(s, 0) + v)

    def __getattr__(self, n):
        return getattr(self._e, n)


def emit_ffn(nc, osem, pfx, xfm, wgT, wuT, wdT, sgT, suT, sdT, yfm, zscr, deps,
             sem_bank=None):
    created = {}
    bases = {}

    with contextlib.ExitStack() as ctx:
        def sb(name, shape, dt):
            return ctx.enter_context(nc.sbuf_tensor(pfx + name, shape, dt))

        def ps(name, shape, dt):
            return ctx.enter_context(nc.psum_tensor(pfx + name, shape, dt))

        def sem(name):
            if sem_bank is not None:
                h = sem_bank[name]
                bases[h] = FFN_FINAL[name]
                return h
            h = osem(pfx + name)
            created[name] = h
            return h

        def dw(eng):
            for s, v in deps:
                eng.wait_ge(s, v)

        xbf = sb("xbf", [128, NDC * TH], BF16)
        asb = sb("asb", [128, NHC * TH], BF16)
        warena = sb("warena", [128, 2 * NHC * 128], BF16)
        ones_c = sb("ones_c", [128, 1], BF16)
        ones_cf = sb("ones_cf", [128, 1], F32)
        ones_r = sb("ones_r", [1, 128], F32)
        sg_s = sb("sg_s", [128, NHC], F32)
        su_s = sb("su_s", [128, NHC], F32)
        sd_s = sb("sd_s", [128, NDC], F32)
        stat = sb("stat", [1, 6 * TH], F32)
        Bg = sb("Bg", [128, TH], F32)
        Ba = sb("Ba", [128, TH], F32)
        t1 = sb("t1", [128, TH], F32)
        t2 = sb("t2", [128, TH], BF16)
        t3 = sb("t3", [128, TH], F32)
        t4 = t1
        zsq = t3
        zsb = sb("zsb", [128, TH], F32)
        xres = sb("xres", [128, TH], F32)
        eps_t = sb("eps_t", [1, 1], F32)
        PA = ps("PA", [128, TH], F32)
        PB = ps("PB", [128, TH], F32)
        PC = ps("PC", [128, TH], F32)
        PD = ps("PD", [128, TH], F32)
        x_ld = sem("x_ld"); sc_ld = sem("sc_ld")
        wg_p = [sem("wg0"), sem("wg1")]; wd_p = [sem("wd0"), sem("wd1")]
        gu_t = sem("gu_t"); v_t1 = sem("v_t1"); gu_v = sem("gu_v"); silu_a = sem("silu_a")
        a_v = sem("a_v"); dn_t = sem("dn_t"); dn_v = sem("dn_v"); z_v = sem("z_v")
        q_a = sem("q_a"); st_t = sem("st_t"); xr_ld = sem("xr_ld"); zst = sem("zst")
        t_ph = sem("t_ph"); a_ph = sem("a_ph"); ln_a = sem("ln_a"); ln_v1 = sem("ln_v1")
        ln_a2 = sem("ln_a2"); ln_v2 = sem("ln_v2"); zl_ld = sem("zl_ld"); ap_v = sem("ap_v")
        yst = sem("yst"); ini_v = sem("ini_v")
        block = ctx.enter_context(nc.Block())
        WG0 = 0
        WG1 = NDC * 128
        WU0 = 2 * NDC * 128
        WU1 = 3 * NDC * 128
        WD0 = 0
        WD1 = NHC * 128
        CS = stat[0:1, 0:TH]
        M1 = stat[0:1, TH:2 * TH]
        M2 = stat[0:1, 2 * TH:3 * TH]
        VAR = stat[0:1, 3 * TH:4 * TH]
        RSTD = stat[0:1, 4 * TH:5 * TH]
        NMB = stat[0:1, 5 * TH:6 * TH]

        def xslice(dc, tt):
            c = dc * TH + tt * 512
            return xbf[:, c:c + 512]

        @block.gpsimd
        def _(g):
            dw(g)
            g = _OffEng(g, bases)
            for th in range(2):
                t0 = th * TH
                if th == 1:
                    g.wait_ge(gu_t, NHC)
                for dc in range(NDC):
                    g.dma_start(out=xbf[:, dc * TH:(dc + 1) * TH],
                                in_=xfm[dc * 128:(dc + 1) * 128, t0:t0 + TH]).then_inc(x_ld, 16)
                for hc in range(NHC):
                    it = th * NHC + hc
                    if th == 1 and hc == 0:
                        g.wait_ge(dn_t, NDC)
                    if it >= 2:
                        g.wait_ge(gu_t, it - 1)
                    c0 = hc * 128
                    wgdst = (WG0, WG1)[hc % 2]
                    wudst = (WU0, WU1)[hc % 2]
                    g.dma_start(
                        out=warena[:, wgdst:wgdst + NDC * 128],
                        in_=wgT[:, c0:c0 + 128].rearrange("(dc p) h -> p dc h", p=128),
                    ).then_inc(wg_p[hc % 2], 16)
                    g.dma_start(
                        out=warena[:, wudst:wudst + NDC * 128],
                        in_=wuT[:, c0:c0 + 128].rearrange("(dc p) h -> p dc h", p=128),
                    ).then_inc(wg_p[hc % 2], 16)
                for dt in range(NDC):
                    it = th * NDC + dt
                    if dt == 0:
                        g.wait_ge(gu_t, 44 * (th + 1))
                    if it >= 2:
                        g.wait_ge(dn_t, it - 1)
                    c0 = dt * 128
                    wddst = (WD0, WD1)[dt % 2]
                    g.dma_start(
                        out=warena[:, wddst:wddst + NHC * 128],
                        in_=wdT[:, c0:c0 + 128].rearrange("(hc p) h -> p hc h", p=128),
                    ).then_inc(wd_p[dt % 2], 16)

        @block.sync
        def _(sq):
            dw(sq)
            sq = _OffEng(sq, bases)
            sq.dma_start(out=sg_s[:, :], in_=sgT[:, :]).then_inc(sc_ld, 16)
            sq.dma_start(out=su_s[:, :], in_=suT[:, :]).then_inc(sc_ld, 16)
            sq.dma_start(out=sd_s[:, :], in_=sdT[:, :]).then_inc(sc_ld, 16)
            for th in range(2):
                t0 = th * TH
                for dt in range(NDC):
                    it = th * NDC + dt
                    if it >= 1:
                        sq.wait_ge(z_v, it)
                    sq.dma_start(
                        out=xres[:, :],
                        in_=xfm[dt * 128:(dt + 1) * 128, t0:t0 + TH]).then_inc(xr_ld, 16)
                    sq.wait_ge(z_v, it + 1)
                    sq.dma_start(
                        out=zscr[dt * 128:(dt + 1) * 128, t0:t0 + TH],
                        in_=zsb[:, :]).then_inc(zst, 16)
                for dt in range(NDC):
                    it = th * NDC + dt
                    if dt == 0:
                        sq.wait_ge(zst, 16 * 16 * (th + 1))
                    if it >= 1:
                        sq.wait_ge(ap_v, 2 * it - 1)
                    sq.dma_start(
                        out=t1[:, :],
                        in_=zscr[dt * 128:(dt + 1) * 128, t0:t0 + TH]).then_inc(zl_ld, 16)
                    sq.wait_ge(ap_v, 2 * (it + 1))
                    sq.dma_start(
                        out=yfm[dt * 128:(dt + 1) * 128, t0:t0 + TH],
                        in_=zsb[:, :]).then_inc(yst, 16)
            sq.wait_ge(yst, 16 * 32)

        @block.vector
        def _(V):
            dw(V)
            V = _OffEng(V, bases)
            V.memset(ones_c[:, :], 1.0)
            V.memset(ones_cf[:, :], 1.0)
            V.memset(ones_r[:, :], 1.0)
            V.memset(eps_t[:, :], EPS_LN).then_inc(ini_v, 1)
            V.wait_ge(sc_ld, 48)
            for th in range(2):
                for hc in range(NHC):
                    it = th * NHC + hc
                    psG = (PA, PC)[hc % 2]
                    psU = (PB, PD)[hc % 2]
                    V.wait_ge(gu_t, it + 1)
                    V.wait_ge(a_ph, 6 * th + 2)
                    if it >= 1:
                        V.wait_ge(silu_a, it)
                    V.tensor_tensor(out=t1[:, :], in0=psG[:, :], in1=Bg[:, :],
                                    op=ALU.add).then_inc(v_t1, 1)
                    if it >= 1:
                        V.wait_ge(a_v, it)
                    V.tensor_tensor(out=t3[:, :], in0=psU[:, :], in1=Bg[:, :],
                                    op=ALU.add).then_inc(gu_v, 1)
                    V.wait_ge(silu_a, it + 1)
                    V.wait_ge(gu_v, it + 1)
                    if th == 1:
                        V.wait_ge(dn_t, NDC)
                    V.scalar_tensor_tensor(
                        out=asb[:, hc * TH:(hc + 1) * TH], in0=t3[:, :],
                        scalar=su_s[:, hc:hc + 1], in1=t2[:, :],
                        op0=ALU.mult, op1=ALU.mult).then_inc(a_v, 1)
                for dt in range(NDC):
                    it = th * NDC + dt
                    psD = (PA, PB)[dt % 2]
                    V.wait_ge(dn_t, it + 1)
                    V.wait_ge(a_ph, 6 * th + 4)
                    V.wait_ge(silu_a, NHC * (th + 1))
                    if it >= 1:
                        V.wait_ge(z_v, it)
                    V.tensor_tensor(out=t4[:, :], in0=psD[:, :], in1=Ba[:, :],
                                    op=ALU.add).then_inc(dn_v, 1)
                    V.wait_ge(xr_ld, 16 * (it + 1))
                    if dt == 0 and th >= 1:
                        V.wait_ge(yst, 16 * 16 * th)
                    if it >= 1:
                        V.wait_ge(st_t, it)
                        V.wait_ge(q_a, it)
                        V.wait_ge(zst, 16 * it)
                    V.wait_ge(dn_v, it + 1)
                    V.scalar_tensor_tensor(
                        out=zsb[:, :], in0=t4[:, :],
                        scalar=sd_s[:, dt:dt + 1], in1=xres[:, :],
                        op0=ALU.mult, op1=ALU.add).then_inc(z_v, 1)
                V.wait_ge(ln_a, th + 1)
                V.tensor_tensor(out=VAR, in0=M1, in1=M1, op=ALU.mult).then_inc(ln_v1, 1)
                V.wait_ge(ln_v1, 3 * th + 1)
                V.tensor_tensor(out=VAR, in0=M2, in1=VAR, op=ALU.subtract).then_inc(ln_v1, 1)
                V.wait_ge(ln_a2, th + 1)
                V.reciprocal(out=VAR, in_=RSTD).then_inc(ln_v1, 1)
                V.wait_ge(ln_v1, 3 * th + 3)
                V.tensor_tensor(out=NMB, in0=M1, in1=VAR, op=ALU.mult).then_inc(ln_v2, 1)
                for dt in range(NDC):
                    it = th * NDC + dt
                    if dt == 0:
                        V.wait_ge(zst, 16 * 16 * (th + 1))
                    V.wait_ge(zl_ld, 16 * (it + 1))
                    V.wait_ge(a_ph, 6 * th + 6)
                    if it >= 1:
                        V.wait_ge(yst, 16 * it)
                        V.wait_ge(ap_v, 2 * it)
                    V.tensor_tensor(out=t3[:, :], in0=t1[:, :], in1=Bg[:, :],
                                    op=ALU.mult).then_inc(ap_v, 1)
                    V.wait_ge(ap_v, 2 * it + 1)
                    V.tensor_tensor(out=zsb[:, :],
                                    in0=t3[:, :], in1=Ba[:, :],
                                    op=ALU.subtract).then_inc(ap_v, 1)

        @block.scalar
        def _(A):
            dw(A)
            A = _OffEng(A, bases)
            A.wait_ge(sc_ld, 48)
            for th in range(2):
                A.wait_ge(t_ph, 6 * th + 1)
                A.activation(out=CS[:, 0:512], in_=PC[0:1, 0:512],
                             func=AF.Identity, scale=-128.0)
                A.activation(out=CS[:, 512:1024], in_=PC[0:1, 512:1024],
                             func=AF.Identity, scale=-128.0).then_inc(a_ph, 1)
                A.wait_ge(t_ph, 6 * th + 2)
                A.activation(out=Bg[:, 0:512], in_=PA[:, 0:512], func=AF.Identity)
                A.activation(out=Bg[:, 512:1024], in_=PA[:, 512:1024],
                             func=AF.Identity).then_inc(a_ph, 1)
                for hc in range(NHC):
                    it = th * NHC + hc
                    A.wait_ge(v_t1, it + 1)
                    if it >= 1:
                        A.wait_ge(a_v, it)
                    A.activation(out=t2[:, :], in_=t1[:, :], func=AF.Silu,
                                 scale=sg_s[:, hc:hc + 1]).then_inc(silu_a, 1)
                A.wait_ge(t_ph, 6 * th + 3)
                A.activation(out=CS[:, 0:512], in_=PA[0:1, 0:512],
                             func=AF.Identity, scale=-128.0)
                A.activation(out=CS[:, 512:1024], in_=PA[0:1, 512:1024],
                             func=AF.Identity, scale=-128.0).then_inc(a_ph, 1)
                A.wait_ge(t_ph, 6 * th + 4)
                A.activation(out=Ba[:, 0:512], in_=PB[:, 0:512], func=AF.Identity)
                A.activation(out=Ba[:, 512:1024], in_=PB[:, 512:1024],
                             func=AF.Identity).then_inc(a_ph, 1)
                for dt in range(NDC):
                    it = th * NDC + dt
                    A.wait_ge(z_v, it + 1)
                    A.wait_ge(a_v, NHC * (th + 1))
                    if it >= 1:
                        A.wait_ge(st_t, it)
                    A.activation(out=zsq[:, :], in_=zsb[:, :],
                                 func=AF.Square).then_inc(q_a, 1)
                A.wait_ge(st_t, 16 * (th + 1))
                A.activation(out=M1[:, 0:512], in_=PC[0:1, 0:512],
                             func=AF.Identity, scale=1.0 / D)
                A.activation(out=M1[:, 512:1024], in_=PC[0:1, 512:1024],
                             func=AF.Identity, scale=1.0 / D)
                A.activation(out=M2[:, 0:512], in_=PD[0:1, 0:512],
                             func=AF.Identity, scale=1.0 / D)
                A.activation(out=M2[:, 512:1024], in_=PD[0:1, 512:1024],
                             func=AF.Identity, scale=1.0 / D).then_inc(ln_a, 1)
                A.wait_ge(ln_v1, 3 * th + 2)
                A.wait_ge(ini_v, 1)
                A.activation(out=RSTD, in_=VAR, func=AF.Sqrt,
                             bias=eps_t[0:1, 0:1]).then_inc(ln_a2, 1)
                A.wait_ge(t_ph, 6 * th + 5)
                A.activation(out=Bg[:, 0:512], in_=PA[:, 0:512], func=AF.Identity)
                A.activation(out=Bg[:, 512:1024], in_=PA[:, 512:1024],
                             func=AF.Identity).then_inc(a_ph, 1)
                A.wait_ge(t_ph, 6 * th + 6)
                A.activation(out=Ba[:, 0:512], in_=PB[:, 0:512], func=AF.Identity)
                A.activation(out=Ba[:, 512:1024], in_=PB[:, 512:1024],
                             func=AF.Identity).then_inc(a_ph, 1)

        @block.tensor
        def _(T):
            dw(T)
            T = _OffEng(T, bases)
            T.wait_ge(ini_v, 1)
            for th in range(2):
                t0 = th * TH
                T.wait_ge(x_ld, 16 * NDC * (th + 1))
                if th == 1:
                    T.wait_ge(ln_a, 1)
                for tt in range(2):
                    for dc in range(NDC):
                        mm = T.matmul(PC[0:1, tt * 512:(tt + 1) * 512], ones_c[:, :],
                                      xslice(dc, tt),
                                      start=(dc == 0), stop=(dc == NDC - 1))
                mm.then_inc(t_ph, 1)
                T.wait_ge(a_ph, 6 * th + 1)
                if th == 1:
                    T.wait_ge(ap_v, 2 * 16 * th)
                for tt in range(2):
                    mm = T.matmul(PA[:, tt * 512:(tt + 1) * 512], ones_r[:, :],
                                  CS[:, tt * 512:(tt + 1) * 512], start=True, stop=True)
                mm.then_inc(t_ph, 1)
                for hc in range(NHC):
                    it = th * NHC + hc
                    psG = (PA, PC)[hc % 2]
                    psU = (PB, PD)[hc % 2]
                    T.wait_ge(wg_p[hc % 2], 32 * (it // 2 + 1))
                    if hc == 0:
                        T.wait_ge(a_ph, 6 * th + 2)
                    if it >= 2:
                        T.wait_ge(gu_v, it - 1)
                    wg = (WG0, WG1)[hc % 2]
                    wu = (WU0, WU1)[hc % 2]
                    for tt in range(2):
                        for dc in range(NDC):
                            T.matmul(psG[:, tt * 512:(tt + 1) * 512],
                                     warena[:, wg + dc * 128:wg + (dc + 1) * 128],
                                     xslice(dc, tt),
                                     start=(dc == 0), stop=(dc == NDC - 1))
                    for tt in range(2):
                        for dc in range(NDC):
                            mm = T.matmul(psU[:, tt * 512:(tt + 1) * 512],
                                          warena[:, wu + dc * 128:wu + (dc + 1) * 128],
                                          xslice(dc, tt),
                                          start=(dc == 0), stop=(dc == NDC - 1))
                    mm.then_inc(gu_t, 1)
                T.wait_ge(a_v, NHC * (th + 1))
                T.wait_ge(gu_v, NHC * (th + 1))
                for tt in range(2):
                    for hc in range(NHC):
                        mm = T.matmul(PA[0:1, tt * 512:(tt + 1) * 512], ones_c[:, :],
                                      asb[:, hc * TH + tt * 512: hc * TH + tt * 512 + 512],
                                      start=(hc == 0), stop=(hc == NHC - 1))
                mm.then_inc(t_ph, 1)
                T.wait_ge(a_ph, 6 * th + 3)
                for tt in range(2):
                    mm = T.matmul(PB[:, tt * 512:(tt + 1) * 512], ones_r[:, :],
                                  CS[:, tt * 512:(tt + 1) * 512], start=True, stop=True)
                mm.then_inc(t_ph, 1)
                def _lnstats(T, th, dt):
                    it = th * NDC + dt
                    T.wait_ge(z_v, it + 1)
                    for tt in range(2):
                        T.matmul(PC[0:1, tt * 512:(tt + 1) * 512], ones_cf[:, :],
                                 zsb[:, tt * 512:tt * 512 + 512],
                                 start=(dt == 0), stop=(dt == NDC - 1))
                    T.wait_ge(q_a, it + 1)
                    for tt in range(2):
                        mm = T.matmul(PD[0:1, tt * 512:(tt + 1) * 512], ones_cf[:, :],
                                      zsq[:, tt * 512:(tt + 1) * 512],
                                      start=(dt == 0), stop=(dt == NDC - 1))
                    mm.then_inc(st_t, 1)

                for dt in range(NDC):
                    it = th * NDC + dt
                    psD = (PA, PB)[dt % 2]
                    T.wait_ge(wd_p[dt % 2], 16 * (it // 2 + 1))
                    if dt == 0:
                        T.wait_ge(a_ph, 6 * th + 4)
                    if it >= 2:
                        T.wait_ge(dn_v, it - 1)
                    wd = (WD0, WD1)[dt % 2]
                    for tt in range(2):
                        for hc in range(NHC):
                            mm = T.matmul(psD[:, tt * 512:(tt + 1) * 512],
                                          warena[:, wd + hc * 128:wd + (hc + 1) * 128],
                                          asb[:, hc * TH + tt * 512: hc * TH + tt * 512 + 512],
                                          start=(hc == 0), stop=(hc == NHC - 1))
                    mm.then_inc(dn_t, 1)
                    if dt >= 1:
                        _lnstats(T, th, dt - 1)
                _lnstats(T, th, NDC - 1)
                T.wait_ge(ln_v2, th + 1)
                T.wait_ge(dn_v, 16 * (th + 1))
                for tt in range(2):
                    mm = T.matmul(PA[:, tt * 512:(tt + 1) * 512], ones_r[:, :],
                                  VAR[:, tt * 512:(tt + 1) * 512], start=True, stop=True)
                mm.then_inc(t_ph, 1)
                T.wait_ge(a_ph, 6 * th + 5)
                for tt in range(2):
                    mm = T.matmul(PB[:, tt * 512:(tt + 1) * 512], ones_r[:, :],
                                  NMB[:, tt * 512:(tt + 1) * 512], start=True, stop=True)
                mm.then_inc(t_ph, 1)

    return [(yst, bases.get(yst, 0) + 16 * 32)], created


def ffn_in_maps(xfm_list, wq_gate, s_gate, wq_up, s_up, wq_down, s_down):
    wgT = np.ascontiguousarray(np.asarray(wq_gate).T)
    wuT = np.ascontiguousarray(np.asarray(wq_up).T)
    wdT = np.ascontiguousarray(np.asarray(wq_down).T)
    sgT = np.ascontiguousarray(np.asarray(s_gate).reshape(NHC, 128).T)
    suT = np.ascontiguousarray(np.asarray(s_up).reshape(NHC, 128).T)
    sdT = np.ascontiguousarray(np.asarray(s_down).reshape(NDC, 128).T)
    return [
        {"xfm": x, "wgT": wgT, "wuT": wuT, "wdT": wdT,
         "sgT": sgT, "suT": suT, "sdT": sdT}
        for x in xfm_list
    ]



def build_conv(nslot):
    nc = bass.Bass("TRN2", target_bir_lowering=False, debug=False)
    x0 = nc.dram_tensor("x0", [NTOK, D], F32, kind="ExternalInput").ap()
    xsrc = nc.dram_tensor("xsrc", [nslot, 64, D], F32, kind="ExternalInput").ap()
    wsl = nc.dram_tensor("wsl", [nslot, 128, 512], F32, kind="ExternalInput").ap()
    idx = nc.dram_tensor("idx", [64, nslot], I32, kind="ExternalInput").ap()
    idf = nc.dram_tensor("idf", [128, 128], F32, kind="ExternalInput").ap()
    x1fm = nc.dram_tensor("x1fm", [D, NTOK], F32, kind="ExternalOutput").ap()
    zdr = nc.dram_tensor("zdr", [NTOK, D], F32).ap()
    with contextlib.ExitStack() as octx:
        def osem(name):
            return octx.enter_context(nc.semaphore(name))
        emit_conv(nc, osem, "C", nslot, x0, xsrc, wsl, idx, idf, x1fm, zdr, [])
    return nc


def emit_conv(nc, osem, pfx, nslot, x0, xsrc, wsl, idx, idf, x1fm, zdr, deps):
    XW = 2080

    with contextlib.ExitStack() as ctx:
        def sb(name, shape, dt):
            return ctx.enter_context(nc.sbuf_tensor(pfx + name, shape, dt))

        def ps(name, shape, dt):
            return ctx.enter_context(nc.psum_tensor(pfx + name, shape, dt))

        def sem(name):
            return osem(pfx + name)

        def dw(eng):
            for s, v in deps:
                eng.wait_ge(s, v)

        xstk = [sb(f"xstk{i}", [128, XW], BF16) for i in range(2)]
        wt = [sb(f"wt{i}", [128, 512], BF16) for i in range(2)]
        slo = [sb(f"slo{i}", [64, D], F32) for i in range(2)]
        idxs = sb("idxs", [64, max(nslot, 1)], I32)
        ident = sb("ident", [128, 128], F32)
        zt = [sb(f"zt{i}", [128, D], F32) for i in range(2)]
        zq = sb("zq", [128, D], F32)
        xln = [sb(f"xln{i}", [128, D], F32) for i in range(2)]
        fmsb = [sb(f"fmsb{i}", [128, D], F32) for i in range(2)]
        st = sb("st", [128, 12], F32)
        eps_t = sb("eps_c", [128, 1], F32)
        PAc = ps("PAc", [128, 2048], F32)
        PBc = ps("PBc", [128, 2048], F32)

        s_ini = sem("s_ini")
        xk = [sem("xk0"), sem("xk1")]
        cv_t = sem("cv_t")
        ev_a = sem("ev_a")
        sc_g = sem("sc_g")
        zld = sem("zld")
        st_v = sem("st_v")
        sq_a = sem("sq_a")
        ap_a = sem("ap_a")
        tr_t = sem("tr_t")
        sr_a = sem("sr_a")
        fm_v = sem("fm_v")
        ost = [sem("ost0"), sem("ost1")]
        ini_v = sem("ini_vc")
        ini_w = sem("ini_wc")

        NR = NTOK // 128
        V_PER_R = 9

        @block_ctx(ctx, nc)
        def _build(block):
            @block.sync
            def _(sq):
                dw(sq)
                sq.dma_start(out=zdr[:, :], in_=x0[:, :]).then_inc(s_ini, 16)
                sq.dma_start(out=idxs[:, 0:nslot], in_=idx[:, :]).then_inc(s_ini, 16)
                sq.dma_start(out=ident[:, :], in_=idf[:, :]).then_inc(s_ini, 16)
                for r in range(NR):
                    sq.wait_ge(sc_g, 16 * nslot)
                    if r >= 2:
                        sq.wait_ge(sq_a, r - 1)
                        sq.wait_ge(ap_a, r - 1)
                    sq.dma_start(out=zt[r % 2][:, :],
                                 in_=zdr[r * 128:(r + 1) * 128, :]).then_inc(zld, 16)
                    sq.wait_ge(fm_v, r + 1)
                    for c in range(NDC):
                        sq.dma_start(
                            out=x1fm[c * 128:(c + 1) * 128, r * 128:(r + 1) * 128],
                            in_=fmsb[r % 2][:, c * 128:(c + 1) * 128]).then_inc(ost[r % 2], 16)
                sq.wait_ge(ost[0], 16 * NDC * NR // 2)
                sq.wait_ge(ost[1], 16 * NDC * NR // 2)

            @block.gpsimd
            def _(g):
                dw(g)
                g.memset(xstk[0][:, :], 0.0)
                g.memset(xstk[1][:, :], 0.0).then_inc(ini_v, 1)
                g.wait_ge(ini_v, 1)
                for s in range(nslot):
                    p = s % 2
                    if s >= 2:
                        g.wait_ge(cv_t, s - 1)
                    g.dma_start(out=xstk[p][0:64, 15:15 + D],
                                in_=xsrc[s]).then_inc(xk[p], 16)
                    g.dma_start(out=xstk[p][64:128, 7:7 + D],
                                in_=xsrc[s]).then_inc(xk[p], 16)
                    g.dma_start(out=wt[p][:, :], in_=wsl[s]).then_inc(xk[p], 16)
                    if s >= 1:
                        g.wait_ge(ev_a, s)
                        g.wait_ge(s_ini, 48)
                        g.wait_ge(sc_g, 16 * (s - 1))
                        g.indirect_dma_start(
                            out=zdr[:, :],
                            out_offset=IndirectOffsetOnAxis(idxs[:, s - 1:s], 0),
                            in_=slo[(s - 1) % 2][:, :],
                            in_offset=None,
                            compute_op=ALU.add).then_inc(sc_g, 16)
                if nslot >= 1:
                    g.wait_ge(ev_a, nslot)
                    g.wait_ge(s_ini, 48)
                    g.wait_ge(sc_g, 16 * (nslot - 1))
                    g.indirect_dma_start(
                        out=zdr[:, :],
                        out_offset=IndirectOffsetOnAxis(idxs[:, nslot - 1:nslot], 0),
                        in_=slo[(nslot - 1) % 2][:, :],
                        in_offset=None,
                        compute_op=ALU.add).then_inc(sc_g, 16)
                    g.wait_ge(sc_g, 16 * nslot)

            @block.tensor
            def _(T):
                dw(T)
                T.wait_ge(ini_v, 1)
                for s in range(nslot):
                    p = s % 2
                    T.wait_ge(xk[p], 48 * (s // 2 + 1))
                    if s >= 2:
                        T.wait_ge(ev_a, s - 1)
                    psl = (PAc, PBc)[p]
                    for mt in range(4):
                        for tap in range(8):
                            mm = T.matmul(
                                psl[0:64, mt * 512:(mt + 1) * 512],
                                wt[p][:, tap * 64:(tap + 1) * 64],
                                xstk[p][:, tap + mt * 512: tap + mt * 512 + 512],
                                start=(tap == 0), stop=(tap == 7))
                    mm.then_inc(cv_t, 1)
                T.wait_ge(s_ini, 48)
                for r in range(NR):
                    T.wait_ge(ap_a, r + 1)
                    if r >= 2:
                        T.wait_ge(fm_v, r - 1)
                    if r < 2:
                        T.wait_ge(ev_a, nslot)
                    pst = (PAc, PBc)[r % 2]
                    for c in range(NDC):
                        mm = T.transpose(pst[:, c * 128:(c + 1) * 128],
                                         xln[r % 2][:, c * 128:(c + 1) * 128],
                                         ident[:, :])
                    mm.then_inc(tr_t, 1)

            @block.scalar
            def _(A):
                dw(A)
                for s in range(nslot):
                    p = s % 2
                    A.wait_ge(cv_t, s + 1)
                    if s >= 2:
                        A.wait_ge(sc_g, 16 * (s - 1))
                    A.activation(out=slo[p][:, :], in_=(PAc, PBc)[p][0:64, :],
                                 func=AF.Identity).then_inc(ev_a, 1)
                A.wait_ge(ini_w, 1)
                for r in range(NR):
                    vb = V_PER_R * r
                    A.wait_ge(zld, 16 * (r + 1))
                    if r >= 1:
                        A.wait_ge(st_v, V_PER_R * (r - 1) + 2)
                    A.activation(out=zq[:, :], in_=zt[r % 2][:, :],
                                 func=AF.Square).then_inc(sq_a, 1)
                    A.wait_ge(st_v, vb + 6)
                    A.activation(out=st[:, 6:7], in_=st[:, 5:6], func=AF.Sqrt,
                                 bias=eps_t[:, 0:1]).then_inc(sr_a, 1)
                    A.wait_ge(st_v, vb + 9)
                    if r >= 2:
                        A.wait_ge(tr_t, r - 1)
                    A.activation(out=xln[r % 2][:, :], in_=zt[r % 2][:, :],
                                 func=AF.Identity, scale=st[:, 7:8],
                                 bias=st[:, 9:10]).then_inc(ap_a, 1)

            @block.vector
            def _(V):
                dw(V)
                V.memset(eps_t[:, :], EPS_LN).then_inc(ini_w, 1)
                for r in range(NR):
                    vb = V_PER_R * r
                    V.wait_ge(zld, 16 * (r + 1))
                    V.reduce_sum(out=st[:, 0:1], in_=zt[r % 2][:, :], axis=AX.X).then_inc(st_v, 1)
                    V.wait_ge(sq_a, r + 1)
                    V.reduce_sum(out=st[:, 1:2], in_=zq[:, :], axis=AX.X).then_inc(st_v, 1)
                    V.wait_ge(st_v, vb + 1)
                    V.tensor_scalar_mul(out=st[:, 2:3], in0=st[:, 0:1], scalar1=1.0 / D).then_inc(st_v, 1)
                    V.wait_ge(st_v, vb + 2)
                    V.tensor_scalar_mul(out=st[:, 3:4], in0=st[:, 1:2], scalar1=1.0 / D).then_inc(st_v, 1)
                    V.wait_ge(st_v, vb + 3)
                    V.tensor_tensor(out=st[:, 4:5], in0=st[:, 2:3], in1=st[:, 2:3], op=ALU.mult).then_inc(st_v, 1)
                    V.wait_ge(st_v, vb + 5)
                    V.tensor_tensor(out=st[:, 5:6], in0=st[:, 3:4], in1=st[:, 4:5], op=ALU.subtract).then_inc(st_v, 1)
                    V.wait_ge(sr_a, r + 1)
                    V.reciprocal(out=st[:, 7:8], in_=st[:, 6:7]).then_inc(st_v, 1)
                    V.wait_ge(st_v, vb + 7)
                    V.tensor_tensor(out=st[:, 8:9], in0=st[:, 2:3], in1=st[:, 7:8], op=ALU.mult).then_inc(st_v, 1)
                    V.wait_ge(st_v, vb + 8)
                    V.tensor_scalar_mul(out=st[:, 9:10], in0=st[:, 8:9], scalar1=-1.0).then_inc(st_v, 1)
                    V.wait_ge(tr_t, r + 1)
                    if r >= 2:
                        V.wait_ge(ost[r % 2], 16 * NDC * (r // 2))
                    for bank in range(4):
                        cp = V.tensor_copy(out=fmsb[r % 2][:, bank * 512:(bank + 1) * 512],
                                           in_=(PAc, PBc)[r % 2][:, bank * 512:(bank + 1) * 512])
                    cp.then_inc(fm_v, 1)

    return [(ost[0], 16 * NDC * NR // 2), (ost[1], 16 * NDC * NR // 2)]


def block_ctx(ctx, nc):
    block = ctx.enter_context(nc.Block())

    def deco(fn):
        fn(block)
        return fn
    return deco


def conv_plan(rows, cols):
    rows = np.asarray(rows)
    cnt = {}
    for j in range(len(rows)):
        cnt.setdefault(int(rows[j]), []).append(j)
    rv = sorted(cnt, key=lambda r: -len(cnt[r]))
    side = {0: [], 1: []}
    load = [0, 0]
    for r in rv:
        s = 0 if (load[0] <= load[1] and len(side[0]) < 32) or len(side[1]) >= 32 else 1
        side[s].append(r)
        load[s] += len(cnt[r])
    rest = [r for r in range(64) if r not in cnt]
    while len(side[0]) < 32:
        side[0].append(rest.pop())
    while len(side[1]) < 32:
        side[1].append(rest.pop())
    slots = {h: [(pos, j) for pos, r in enumerate(side[h]) for j in cnt.get(r, [])]
             for h in (0, 1)}
    nslot = max(len(slots[0]), len(slots[1]), 1)
    nslot = (nslot + 7) // 8 * 8
    return side, slots, nslot


def conv_stage_half(x0b, block_values, cols, side_rows, side_slots, nslot):
    cols = np.asarray(cols)
    bv = np.asarray(block_values)
    x0b = np.asarray(x0b)
    x0_own = np.ascontiguousarray(
        x0b.reshape(64, 64, D)[side_rows].reshape(NTOK, D)).astype(np.float32)
    xsrc = np.zeros((nslot, 64, D), np.float32)
    wsl = np.zeros((nslot, 128, 512), np.float32)
    idx = np.zeros((64, nslot), np.int32)
    for s, (pos, j) in enumerate(side_slots):
        c = int(cols[j])
        xsrc[s] = x0b[c * 64:(c + 1) * 64, :]
        arr = bv[j].transpose(1, 2, 0)
        wsl[s] = np.concatenate([arr[:, :8, :], arr[:, 8:, :]], axis=0).reshape(128, 512)
        idx[:, s] = pos * 64 + np.arange(64)
    return {"x0": x0_own, "xsrc": xsrc, "wsl": wsl, "idx": idx,
            "idf": np.eye(128, dtype=np.float32)}



def build_kv():
    nc = bass.Bass("TRN2", target_bir_lowering=False, debug=False)
    xfm = nc.dram_tensor("xfm", [D, NTOK], F32, kind="ExternalInput").ap()
    Wk = nc.dram_tensor("Wk", [D, 2 * DH * NHEAD], F32, kind="ExternalInput").ap()
    Wv = nc.dram_tensor("Wv", [D, D], F32, kind="ExternalInput").ap()
    kout = nc.dram_tensor("kout", [2 * DH * NHEAD, NTOK], BF16, kind="ExternalOutput").ap()
    vout = nc.dram_tensor("vout", [NTOK, D], BF16, kind="ExternalOutput").ap()
    with contextlib.ExitStack() as octx:
        def osem(name):
            return octx.enter_context(nc.semaphore(name))
        emit_kv(nc, osem, "K", xfm, Wk, Wv, kout, vout, [])
    return nc


def emit_kv(nc, osem, pfx, xfm, Wk, Wv, kout, vout, deps):
    NKT = 2 * DH * NHEAD // 128

    with contextlib.ExitStack() as ctx:
        def sb(name, shape, dt):
            return ctx.enter_context(nc.sbuf_tensor(pfx + name, shape, dt))

        def ps(name, shape, dt):
            return ctx.enter_context(nc.psum_tensor(pfx + name, shape, dt))

        def sem(name):
            return osem(pfx + name)

        def dw(eng):
            for s, v in deps:
                eng.wait_ge(s, v)

        xbf = sb("xbf", [128, NDC * NTOK], BF16)
        wvbf = sb("wvbf", [128, NDC * D], BF16)
        wk_t = [sb(f"wk{i}", [128, NDC * 128], BF16) for i in range(2)]
        kbf = [sb(f"kbf{i}", [128, NTOK], BF16) for i in range(2)]
        vbf = [sb(f"vbf{i}", [128, D], BF16) for i in range(2)]
        PS0 = ps("PS0", [128, 2048], F32)
        PS1 = ps("PS1", [128, 2048], F32)

        x_ld = sem("x_ld")
        wv_ld = sem("wv_ld")
        wk_p = [sem("wk0"), sem("wk1")]
        mm_t = sem("mm_t")
        ev_a = sem("ev_a")
        ost = [sem("ostk0"), sem("ostk1")]

        @block_ctx(ctx, nc)
        def _build(block):
            @block.gpsimd
            def _(g):
                dw(g)
                for dc in range(NDC):
                    g.dma_start(out=xbf[:, dc * NTOK:(dc + 1) * NTOK],
                                in_=xfm[dc * 128:(dc + 1) * 128, :]).then_inc(x_ld, 16)
                for ht in range(NKT):
                    if ht >= 2:
                        g.wait_ge(mm_t, ht - 1)
                    g.dma_start(
                        out=wk_t[ht % 2][:, :],
                        in_=Wk[:, ht * 128:(ht + 1) * 128].rearrange(
                            "(dc p) h -> p dc h", p=128)).then_inc(wk_p[ht % 2], 16)
                g.wait_ge(mm_t, NKT - 1)
                for dc in range(NDC):
                    g.dma_start(out=wvbf[:, dc * D:(dc + 1) * D],
                                in_=Wv[dc * 128:(dc + 1) * 128, :]).then_inc(wv_ld, 16)

            @block.tensor
            def _(T):
                dw(T)
                T.wait_ge(x_ld, 16 * NDC)
                for ht in range(NKT):
                    psK = (PS0, PS1)[ht % 2]
                    T.wait_ge(wk_p[ht % 2], 16 * (ht // 2 + 1))
                    if ht >= 2:
                        T.wait_ge(ev_a, ht - 1)
                    for tt in range(4):
                        for dc in range(NDC):
                            mm = T.matmul(psK[:, tt * 512:(tt + 1) * 512],
                                          wk_t[ht % 2][:, dc * 128:(dc + 1) * 128],
                                          xbf[:, dc * NTOK + tt * 512: dc * NTOK + tt * 512 + 512],
                                          start=(dc == 0), stop=(dc == NDC - 1))
                    mm.then_inc(mm_t, 1)
                T.wait_ge(wv_ld, 16 * NDC)
                for tq in range(NDC):
                    it = NKT + tq
                    psV = (PS0, PS1)[tq % 2]
                    if it >= 2:
                        T.wait_ge(ev_a, it - 1)
                    for hv in range(4):
                        for dc in range(NDC):
                            mm = T.matmul(psV[:, hv * 512:(hv + 1) * 512],
                                          xbf[:, dc * NTOK + tq * 128: dc * NTOK + tq * 128 + 128],
                                          wvbf[:, dc * D + hv * 512: dc * D + hv * 512 + 512],
                                          start=(dc == 0), stop=(dc == NDC - 1))
                    mm.then_inc(mm_t, 1)

            @block.scalar
            def _(A):
                dw(A)
                for it in range(NKT + NDC):
                    p = it % 2
                    A.wait_ge(mm_t, it + 1)
                    if it >= 2:
                        A.wait_ge(ost[p], 16 * (it // 2))
                    dst = kbf[p] if it < NKT else vbf[p]
                    A.activation(out=dst[:, :], in_=(PS0, PS1)[p][:, :],
                                 func=AF.Identity).then_inc(ev_a, 1)

            @block.sync
            def _(sq):
                dw(sq)
                for it in range(NKT + NDC):
                    p = it % 2
                    sq.wait_ge(ev_a, it + 1)
                    if it < NKT:
                        sq.dma_start(out=kout[it * 128:(it + 1) * 128, :],
                                     in_=kbf[p][:, :]).then_inc(ost[p], 16)
                    else:
                        tq = it - NKT
                        sq.dma_start(out=vout[tq * 128:(tq + 1) * 128, :],
                                     in_=vbf[p][:, :]).then_inc(ost[p], 16)
                sq.wait_ge(ost[0], 16 * ((NKT + NDC + 1) // 2))
                sq.wait_ge(ost[1], 16 * ((NKT + NDC) // 2))
    return [(ost[0], 16 * ((NKT + NDC + 1) // 2)), (ost[1], 16 * ((NKT + NDC) // 2))]



def build_attn():
    nc = bass.Bass("TRN2", target_bir_lowering=False, debug=False)
    xfm = nc.dram_tensor("xfm", [D, NTOK], F32, kind="ExternalInput").ap()
    kfull = nc.dram_tensor("kfull", [2 * DH * NHEAD, SFULL], BF16, kind="ExternalInput").ap()
    vfull = nc.dram_tensor("vfull", [SFULL, D], BF16, kind="ExternalInput").ap()
    Wq = nc.dram_tensor("Wq", [D, 2 * DH * NHEAD], F32, kind="ExternalInput").ap()
    Wo = nc.dram_tensor("Wo", [D, D], F32, kind="ExternalInput").ap()
    nsT = nc.dram_tensor("nsT", [128, NDC], F32, kind="ExternalInput").ap()
    lam = nc.dram_tensor("lam", [1, NHEAD], F32, kind="ExternalInput").ap()
    idf = nc.dram_tensor("idf", [128, 128], F32, kind="ExternalInput").ap()
    x3fm = nc.dram_tensor("x3fm", [D, NTOK], F32, kind="ExternalOutput").ap()
    oscr = nc.dram_tensor("oscr", [D, NTOK], BF16).ap()
    zscr = nc.dram_tensor("zscr3", [D, NTOK], F32).ap()
    kv_srcs = ([kfull[:, 0:NTOK], kfull[:, NTOK:SFULL]],
               [vfull[0:NTOK, :], vfull[NTOK:SFULL, :]])
    with contextlib.ExitStack() as octx:
        def osem(name):
            return octx.enter_context(nc.semaphore(name))
        emit_attn(nc, osem, "A", xfm, kv_srcs, Wq, Wo, nsT, lam, idf, x3fm,
                  oscr, zscr, [])
    return nc


def emit_attn(nc, osem, pfx, xfm, kv_srcs, Wq, Wo, nsT, lam, idf, x3fm,
              oscr, zscr, deps, kv_gate=()):
    khalf, vhalf = kv_srcs
    NST = 16
    ISC = 1.0 / float(np.sqrt(DH))

    TB = 1
    AB = 1
    VB = 3

    tmap = {}
    _v = 2
    for it in range(17):
        if it <= 15:
            for k in range(8):
                _v += 1
                tmap[("s", it, k)] = _v
        if it >= 1:
            for g in range(4):
                _v += 1
                tmap[("tr", it - 1, g)] = _v
            if (it - 1) % 4 == 3:
                _v += 1
                tmap[("av", (it - 1) // 4)] = _v
    assert _v == 198

    def T_q(h, half):
        return TB + 198 * h + half + 1

    def T_s(h, st, k):
        return TB + 198 * h + tmap[("s", st, k)]

    def T_tr(h, st, g):
        return TB + 198 * h + tmap[("tr", st, g)]

    def T_av(h, sb):
        return TB + 198 * h + tmap[("av", sb)]

    amap = {}
    _a = 2
    for it in range(17):
        if it <= 15:
            for k in range(8):
                _a += 1
                amap[("e", it, k)] = _a
        if it >= 1:
            for g in range(4):
                _a += 1
                amap[("x3", it - 1, g)] = _a
            if (it - 1) % 4 == 3:
                _a += 1
                amap[("av", (it - 1) // 4)] = _a
    assert _a == 198

    def A_q(h):
        return AB + 198 * h + 2

    def A_ek(h, st, k):
        return AB + 198 * h + amap[("e", st, k)]

    def A_e3(h, st, g):
        return AB + 198 * h + amap[("x3", st, g)]

    def A_av(h, sb):
        return AB + 198 * h + amap[("av", sb)]

    def V_i(h, st, k):
        return VB + 176 * h + 11 * st + k

    T2 = TB + 198 * NHEAD
    A2 = AB + 198 * NHEAD
    V2 = VB + 176 * NHEAD

    with contextlib.ExitStack() as ctx:
        def sb(name, shape, dt):
            return ctx.enter_context(nc.sbuf_tensor(pfx + name, shape, dt))

        def ps(name, shape, dt):
            return ctx.enter_context(nc.psum_tensor(pfx + name, shape, dt))

        def sem(name):
            return osem(pfx + name)

        def dw(eng):
            for s, v in deps:
                eng.wait_ge(s, v)

        xbf = sb("xbf", [128, NDC * NTOK], BF16)
        k_sb = [sb(f"ksb{i}", [128, 2 * SFULL], BF16) for i in range(2)]
        v_sb = sb("vsb", [128, SFULL], BF16)
        q_sb = sb("q_sb", [128, 2 * NTOK], BF16)
        wq_t = sb("wqt", [128, NDC * 256], BF16)
        e1 = [sb(f"e1{i}", [128, SFULL], BF16) for i in range(2)]
        e2 = [sb(f"e2{i}", [128, SFULL], BF16) for i in range(2)]
        t1e = sb("t1e", [128, SFULL], BF16)
        dsb = sb("dsb", [128, SFULL], BF16)
        w3uT = sb("w3uT", [128, 32 * 512], BF16)
        pacc = sb("pacc", [128, 40], F32)
        lamb = sb("lamb", [128, NHEAD], F32)
        lambn = sb("lambn", [128, NHEAD], F32)
        lamr = sb("lamr", [1, NHEAD], F32)
        sft = sb("sft", [1, 2], F32)
        ns_s = sb("ns_s", [128, NDC], F32)
        ones_rf = sb("ones_rf", [1, 128], F32)
        ones_cf = sb("ones_cfa", [128, 1], F32)
        eps_r = sb("eps_r", [1, 1], F32)
        eps_l = sb("eps_l", [1, 1], F32)
        o_sb = [sb(f"osb{i}", [128, 512], BF16) for i in range(2)]
        idb = sb("idb", [128, 128], BF16)
        statA = k_sb[0][:, 0:4096].bitcast(F32)
        statB = k_sb[0][:, 4096:8192].bitcast(F32)
        statC = k_sb[1][:, 0:4096].bitcast(F32)
        Rsb = k_sb[1][:, 4096:8192].bitcast(F32)
        wo_t = [dsb[:, 0:2048], dsb[:, 2048:4096]]
        obuf = [e1[0][:, 0:2048], e1[1][:, 0:2048]]
        oq = v_sb[:, :].bitcast(F32)
        xres = [e2[0][:, :].bitcast(F32), e2[1][:, :].bitcast(F32)]
        z3sb = [w3uT[:, 4096:8192].bitcast(F32), w3uT[:, 8192:12288].bitcast(F32)]
        M1r = statA[0:1, :]
        M2r = statB[0:1, :]
        VARr = statC[0:1, :]
        SVARr = q_sb.ap().bitcast(F32)[0:1, 0:2048]
        RSTDr = wq_t.ap().bitcast(F32)[0:1, 0:2048]
        NMBr = t1e.ap().bitcast(F32)[0:1, 0:2048]
        PS = ps("PS", [128, 4096], F32)
        PST = PS[:, 2048:3072].bitcast(BF16)

        t_sc = sem("t_sc")
        x_ld = sem("x_lda")
        a_ex = sem("a_ex")
        v_cb = sem("v_cb")
        ini = sem("ini")
        wq_ld = sem("wqld")
        kv_p = [sem("kvp0"), sem("kvp1")]
        v_ld = sem("v_ld")
        ost_o = [sem("osto0"), sem("osto1")]
        o_ld = [sem("old0"), sem("old1")]
        o2_ld = [sem("o2ld0"), sem("o2ld1")]
        wo_p = [sem("wop0"), sem("wop1")]
        xr_ld = [sem("xrld0"), sem("xrld1")]
        z3st = [sem("z3st0"), sem("z3st1")]
        zld2 = [sem("zld20"), sem("zld21")]
        ostx = [sem("ostx0"), sem("ostx1")]
        ini_v = sem("ini_va")

        def w3v(g, stq):
            return w3uT.ap().rearrange("p (tc s) -> p tc s", s=512)[
                :, g * 8:(g + 1) * 8, stq * 128:(stq + 1) * 128]

        @block_ctx(ctx, nc)
        def _build(block):
            @block.sync
            def _(sq):
                dw(sq)
                sq.dma_start(out=lamr[:, :], in_=lam[:, :]).then_inc(ini, 16)
                sq.dma_start(out=ns_s[:, :], in_=nsT[:, :]).then_inc(ini, 16)
                for s, v in kv_gate:
                    sq.wait_ge(s, v)
                for h in range(NHEAD):
                    p = h % 2
                    if h >= 1:
                        for s_b in range(4):
                            i = (h - 1) * 4 + s_b
                            sq.wait_ge(a_ex, A_av(h - 1, s_b))
                            sq.dma_start(
                                out=oscr[(h - 1) * 128:h * 128, s_b * 512:(s_b + 1) * 512],
                                in_=o_sb[i % 2][:, :]).then_inc(ost_o[i % 2], 16)
                    if h >= 2:
                        sq.wait_ge(t_sc, T_s(h - 2, 15, 7))
                    for hv, ksrc in enumerate(khalf):
                        sq.dma_start(out=k_sb[p][:, hv * NTOK:(hv + 1) * NTOK],
                                     in_=ksrc[h * 256:h * 256 + 128, :]).then_inc(kv_p[p], 16)
                        sq.dma_start(out=k_sb[p][:, SFULL + hv * NTOK:SFULL + (hv + 1) * NTOK],
                                     in_=ksrc[h * 256 + 128:h * 256 + 256, :]).then_inc(kv_p[p], 16)
                    if h >= 1:
                        sq.wait_ge(t_sc, T_av(h - 1, 3))
                    for hv in range(2):
                        sq.dma_start(
                            out=v_sb[:, hv * NTOK:(hv + 1) * NTOK],
                            in_=vhalf[hv][:, h * 128:(h + 1) * 128].rearrange(
                                "(tc p) d -> p tc d", p=128)).then_inc(v_ld, 16)
                for s_b in range(4):
                    i = 15 * 4 + s_b
                    sq.wait_ge(a_ex, A_av(15, s_b))
                    sq.dma_start(out=oscr[15 * 128:16 * 128, s_b * 512:(s_b + 1) * 512],
                                 in_=o_sb[i % 2][:, :]).then_inc(ost_o[i % 2], 16)
                sq.wait_ge(ost_o[0], 16 * 32)
                sq.wait_ge(ost_o[1], 16 * 32)
                for c in range(NDC):
                    if c >= 2:
                        sq.wait_ge(a_ex, A2 + c - 1)
                    sq.dma_start(out=obuf[c % 2][:, :],
                                 in_=oscr[c * 128:(c + 1) * 128, :]).then_inc(o_ld[c % 2], 16)
                sq.wait_ge(a_ex, A2 + 16)
                for c in range(NDC):
                    if c >= 2:
                        sq.wait_ge(v_cb, V2 + c + 1)
                    sq.dma_start(out=obuf[c % 2][:, :],
                                 in_=oscr[c * 128:(c + 1) * 128, :]).then_inc(o2_ld[c % 2], 16)
                for dt in range(NDC):
                    if dt >= 2:
                        sq.wait_ge(v_cb, V2 + 18 + dt - 1)
                    sq.dma_start(out=xres[dt % 2][:, :],
                                 in_=xfm[dt * 128:(dt + 1) * 128, :]).then_inc(xr_ld[dt % 2], 16)
                    sq.wait_ge(v_cb, V2 + 18 + dt + 1)
                    sq.dma_start(out=zscr[dt * 128:(dt + 1) * 128, :],
                                 in_=z3sb[dt % 2][:, :]).then_inc(z3st[dt % 2], 16)
                for c in range(NDC):
                    if c == 0:
                        sq.wait_ge(z3st[0], 16 * 8)
                        sq.wait_ge(z3st[1], 16 * 8)
                    if c >= 2:
                        sq.wait_ge(a_ex, A2 + 18 + c - 1)
                        sq.wait_ge(t_sc, T2 + 32 + c)
                    sq.dma_start(out=z3sb[c % 2][:, :],
                                 in_=zscr[c * 128:(c + 1) * 128, :]).then_inc(zld2[c % 2], 16)
                for c in range(NDC):
                    if c >= 2:
                        sq.wait_ge(v_cb, V2 + 38 + 2 * (c - 1))
                    sq.dma_start(out=xres[c % 2][:, :],
                                 in_=zscr[c * 128:(c + 1) * 128, :]).then_inc(xr_ld[c % 2], 16)
                    sq.wait_ge(v_cb, V2 + 38 + 2 * (c + 1))
                    sq.dma_start(out=x3fm[c * 128:(c + 1) * 128, :],
                                 in_=z3sb[c % 2][:, :]).then_inc(ostx[c % 2], 16)
                sq.wait_ge(ostx[0], 16 * 8)
                sq.wait_ge(ostx[1], 16 * 8)

            @block.gpsimd
            def _(g):
                dw(g)
                g.dma_start(out=idb[:, :], in_=idf[:, :]).then_inc(x_ld, 16)
                for dc in range(NDC):
                    g.dma_start(out=xbf[:, dc * NTOK:(dc + 1) * NTOK],
                                in_=xfm[dc * 128:(dc + 1) * 128, :]).then_inc(x_ld, 16)
                for h in range(NHEAD):
                    if h >= 1:
                        g.wait_ge(t_sc, T_q(h - 1, 1))
                    g.dma_start(
                        out=wq_t[:, :],
                        in_=Wq[:, h * 256:(h + 1) * 256].rearrange(
                            "(dc p) q -> p dc q", p=128)).then_inc(wq_ld, 16)
                for dt in range(NDC):
                    if dt == 0:
                        g.wait_ge(t_sc, T2)
                    if dt >= 2:
                        g.wait_ge(t_sc, T2 + 16 + dt)
                    g.dma_start(
                        out=wo_t[dt % 2][:, :],
                        in_=Wo[:, dt * 128:(dt + 1) * 128].rearrange(
                            "(dc p) o -> p dc o", p=128)).then_inc(wo_p[dt % 2], 16)

            @block.tensor
            def _(T):
                dw(T)
                T.wait_ge(ini, 32)
                T.wait_ge(ini_v, 1)
                T.wait_ge(x_ld, 16 * (NDC + 1))
                mm = T.matmul(PS[:, 0:16], ones_rf[:, :], lamr[:, :], start=True, stop=True)
                mm.then_inc(t_sc, 1)
                for h in range(NHEAD):
                    p = h % 2
                    T.wait_ge(wq_ld, 16 * (h + 1))
                    if h >= 1:
                        T.wait_ge(a_ex, A_av(h - 1, 3))
                    else:
                        T.wait_ge(a_ex, 1)
                    for half in range(2):
                        for tq in range(4):
                            for dc in range(NDC):
                                mm = T.matmul(
                                    PS[:, half * 2048 + tq * 512: half * 2048 + (tq + 1) * 512],
                                    wq_t[:, dc * 256 + half * 128: dc * 256 + (half + 1) * 128],
                                    xbf[:, dc * NTOK + tq * 512: dc * NTOK + (tq + 1) * 512],
                                    start=(dc == 0), stop=(dc == NDC - 1))
                        mm.then_inc(t_sc, 1)
                    T.wait_ge(kv_p[p], 64 * (h // 2 + 1))
                    for it in range(17):
                        if it <= 15:
                            st = it
                            s_b, stq = st // 4, st % 4
                            for k in range(8):
                                se, q = k // 4 + 1, k % 4
                                if k >= 2:
                                    T.wait_ge(a_ex, A_ek(h, st, k - 2))
                                elif st == 0:
                                    T.wait_ge(a_ex, A_q(h))
                                else:
                                    T.wait_ge(a_ex, A_ek(h, st - 1, k + 6))
                                lq = q_sb[:, (se - 1) * NTOK + s_b * 512 + stq * 128:
                                          (se - 1) * NTOK + s_b * 512 + (stq + 1) * 128]
                                pq = PS[:, (q % 2) * 1024:(q % 2) * 1024 + 1024]
                                for j in range(2):
                                    mm = T.matmul(
                                        pq[:, j * 512:(j + 1) * 512],
                                        lq,
                                        k_sb[p][:, (se - 1) * SFULL + q * 1024 + j * 512:
                                                (se - 1) * SFULL + q * 1024 + (j + 1) * 512],
                                        start=True, stop=True)
                                mm.then_inc(t_sc, 1)
                        if it >= 1:
                            st = it - 1
                            s_b, stq = st // 4, st % 4
                            T.wait_ge(v_cb, V_i(h, st, 11))
                            for g_ in range(4):
                                if g_ >= 2:
                                    T.wait_ge(a_ex, A_e3(h, st, g_ - 2))
                                elif st == 0:
                                    T.wait_ge(a_ex, A_q(h))
                                else:
                                    T.wait_ge(a_ex, A_e3(h, st - 1, g_ + 2))
                                for j in range(8):
                                    tc = g_ * 8 + j
                                    mm = T.transpose(
                                        PST[:, ((g_ % 2) * 8 + j) * 128:((g_ % 2) * 8 + j + 1) * 128],
                                        dsb[:, tc * 128:(tc + 1) * 128], idb[:, :])
                                mm.then_inc(t_sc, 1)
                            if stq == 3:
                                T.wait_ge(a_ex, A_e3(h, st, 3))
                                if s_b >= 1:
                                    T.wait_ge(a_ex, A_av(h, s_b - 1))
                                T.wait_ge(v_ld, 32 * (h + 1))
                                for tc in range(32):
                                    mm = T.matmul(PS[:, 3072:3584],
                                                  v_sb[:, tc * 128:(tc + 1) * 128],
                                                  w3uT[:, tc * 512:(tc + 1) * 512],
                                                  start=(tc == 0), stop=(tc == 31))
                                mm.then_inc(t_sc, 1)
                for c in range(NDC):
                    T.wait_ge(a_ex, A2 + c + 1)
                    for tt in range(4):
                        mm = T.matmul(PS[0:1, tt * 1024: tt * 1024 + 512], ones_cf[:, :],
                                      oq[:, tt * 512:(tt + 1) * 512],
                                      start=(c == 0), stop=(c == NDC - 1))
                    mm.then_inc(t_sc, 1)
                T.wait_ge(a_ex, A2 + 17)
                T.wait_ge(v_cb, V2 + 2)
                for tt in range(4):
                    mm = T.matmul(PS[:, tt * 1024: tt * 1024 + 512], ones_rf[:, :],
                                  VARr[:, tt * 512:(tt + 1) * 512], start=True, stop=True)
                mm.then_inc(t_sc, 1)
                T.wait_ge(a_ex, A2 + 18)
                for dt in range(NDC):
                    T.wait_ge(wo_p[dt % 2], 16 * (dt // 2 + 1))
                    T.wait_ge(v_cb, V2 + 2 + 16)
                    if dt >= 2:
                        T.wait_ge(v_cb, V2 + 18 + dt - 1)
                    psD = (PS[:, 0:2048], PS[:, 2048:4096])[dt % 2]
                    for tt in range(4):
                        for cc in range(NDC):
                            mm = T.matmul(psD[:, tt * 512:(tt + 1) * 512],
                                          wo_t[dt % 2][:, cc * 128:(cc + 1) * 128],
                                          xbf[:, cc * NTOK + tt * 512: cc * NTOK + (tt + 1) * 512],
                                          start=(cc == 0), stop=(cc == NDC - 1))
                    mm.then_inc(t_sc, 1)
                for c in range(NDC):
                    T.wait_ge(zld2[c % 2], 16 * (c // 2 + 1))
                    for tt in range(4):
                        mm = T.matmul(PS[0:1, tt * 1024: tt * 1024 + 512], ones_cf[:, :],
                                      z3sb[c % 2][:, tt * 512:(tt + 1) * 512],
                                      start=(c == 0), stop=(c == NDC - 1))
                    T.wait_ge(a_ex, A2 + 18 + c + 1)
                    for tt in range(4):
                        mm = T.matmul(PS[0:1, 512 + tt * 1024: 512 + tt * 1024 + 512],
                                      ones_cf[:, :],
                                      oq[:, tt * 512:(tt + 1) * 512],
                                      start=(c == 0), stop=(c == NDC - 1))
                    mm.then_inc(t_sc, 1)
                T.wait_ge(v_cb, V2 + 37)
                for tt in range(4):
                    mm = T.matmul(PS[:, tt * 1024: tt * 1024 + 512], ones_rf[:, :],
                                  RSTDr[:, tt * 512:(tt + 1) * 512], start=True, stop=True)
                mm.then_inc(t_sc, 1)
                T.wait_ge(v_cb, V2 + 38)
                for tt in range(4):
                    mm = T.matmul(PS[:, 512 + tt * 1024: 512 + tt * 1024 + 512], ones_rf[:, :],
                                  NMBr[:, tt * 512:(tt + 1) * 512], start=True, stop=True)
                mm.then_inc(t_sc, 1)

            @block.scalar
            def _(A):
                dw(A)
                A.wait_ge(t_sc, 1)
                A.activation(out=lamb[:, :], in_=PS[:, 0:16], func=AF.Identity).then_inc(a_ex, 1)
                for h in range(NHEAD):
                    if h >= 1:
                        A.wait_ge(t_sc, T_s(h - 1, 15, 7))
                    A.wait_ge(t_sc, T_q(h, 0))
                    A.activation(out=q_sb[:, 0:NTOK], in_=PS[:, 0:2048],
                                 func=AF.Identity).then_inc(a_ex, 1)
                    A.wait_ge(t_sc, T_q(h, 1))
                    A.activation(out=q_sb[:, NTOK:2 * NTOK], in_=PS[:, 2048:4096],
                                 func=AF.Identity).then_inc(a_ex, 1)
                    for it in range(17):
                        if it <= 15:
                            st = it
                            pb = (st % 2) * 20
                            A.wait_ge(v_cb, max(0, V_i(h, st - 2, 11)))
                            for k in range(8):
                                se, q = k // 4 + 1, k % 4
                                edst = (e1, e2)[se - 1][st % 2]
                                A.wait_ge(t_sc, T_s(h, st, k))
                                A.activation(out=edst[:, q * 1024:(q + 1) * 1024],
                                             in_=PS[:, (q % 2) * 1024:(q % 2) * 1024 + 1024],
                                             func=AF.Exp, scale=ISC,
                                             accum_out=pacc[:, pb + k:pb + k + 1]).then_inc(a_ex, 1)
                        if it >= 1:
                            st = it - 1
                            s_b, stq = st // 4, st % 4
                            for g_ in range(4):
                                A.wait_ge(t_sc, T_tr(h, st, g_))
                                A.activation(out=w3v(g_, stq),
                                             in_=PST[:, (g_ % 2) * 1024:(g_ % 2) * 1024 + 1024],
                                             func=AF.Exp).then_inc(a_ex, 1)
                            if stq == 3:
                                i = h * 4 + s_b
                                A.wait_ge(t_sc, T_av(h, s_b))
                                if i >= 2:
                                    A.wait_ge(ost_o[i % 2], 16 * (i // 2))
                                A.activation(out=o_sb[i % 2][:, :], in_=PS[:, 3072:3584],
                                             func=AF.Identity).then_inc(a_ex, 1)
                for c in range(NDC):
                    A.wait_ge(o_ld[c % 2], 16 * (c // 2 + 1))
                    if c >= 1:
                        A.wait_ge(t_sc, T2 + c)
                    A.activation(out=oq[:, :], in_=obuf[c % 2][:, :],
                                 func=AF.Square).then_inc(a_ex, 1)
                A.wait_ge(t_sc, T2 + 16)
                A.wait_ge(ini_v, 1)
                for tt in range(4):
                    aa = A.activation(out=M1r[:, tt * 512:(tt + 1) * 512],
                                      in_=PS[0:1, tt * 1024: tt * 1024 + 512],
                                      func=AF.Sqrt, scale=1.0 / D, bias=eps_r[0:1, 0:1])
                aa.then_inc(a_ex, 1)
                A.wait_ge(t_sc, T2 + 17)
                for tt in range(4):
                    aa = A.activation(out=Rsb[:, tt * 512:(tt + 1) * 512],
                                      in_=PS[:, tt * 1024: tt * 1024 + 512],
                                      func=AF.Identity)
                aa.then_inc(a_ex, 1)
                for c in range(NDC):
                    A.wait_ge(zld2[c % 2], 16 * (c // 2 + 1))
                    if c >= 1:
                        A.wait_ge(t_sc, T2 + 33 + c)
                    A.activation(out=oq[:, :], in_=z3sb[c % 2][:, :],
                                 func=AF.Square).then_inc(a_ex, 1)
                A.wait_ge(t_sc, T2 + 49)
                for tt in range(4):
                    aa = A.activation(out=M1r[:, tt * 512:(tt + 1) * 512],
                                      in_=PS[0:1, tt * 1024: tt * 1024 + 512],
                                      func=AF.Identity, scale=1.0 / D)
                for tt in range(4):
                    aa = A.activation(out=M2r[:, tt * 512:(tt + 1) * 512],
                                      in_=PS[0:1, 512 + tt * 1024: 512 + tt * 1024 + 512],
                                      func=AF.Identity, scale=1.0 / D)
                aa.then_inc(a_ex, 1)
                A.wait_ge(v_cb, V2 + 36)
                A.activation(out=SVARr, in_=VARr, func=AF.Sqrt,
                             bias=eps_l[0:1, 0:1]).then_inc(a_ex, 1)
                A.wait_ge(t_sc, T2 + 50)
                for tt in range(4):
                    aa = A.activation(out=Rsb[:, tt * 512:(tt + 1) * 512],
                                      in_=PS[:, tt * 1024: tt * 1024 + 512],
                                      func=AF.Identity)
                aa.then_inc(a_ex, 1)
                A.wait_ge(t_sc, T2 + 51)
                for tt in range(4):
                    aa = A.activation(out=oq[:, tt * 512:(tt + 1) * 512],
                                      in_=PS[:, 512 + tt * 1024: 512 + tt * 1024 + 512],
                                      func=AF.Identity)
                aa.then_inc(a_ex, 1)

            @block.vector
            def _(V):
                dw(V)
                V.memset(ones_rf[:, :], 1.0)
                V.memset(ones_cf[:, :], 1.0)
                V.memset(eps_r[:, :], EPS_RMS)
                V.memset(eps_l[:, :], EPS_LN).then_inc(ini_v, 1)
                V.wait_ge(a_ex, 1)
                V.tensor_scalar_mul(out=lambn[:, :], in0=lamb[:, :], scalar1=-1.0).then_inc(v_cb, 1)
                V.wait_ge(ini, 32)
                V.reduce_sum(out=sft[:, 0:1], in_=lamr[:, :], axis=AX.X).then_inc(v_cb, 1)
                V.wait_ge(v_cb, 2)
                V.tensor_scalar(out=sft[:, 1:2], in0=sft[:, 0:1], scalar1=-1.0 / NHEAD,
                                scalar2=1.0, op0=ALU.mult, op1=ALU.add).then_inc(v_cb, 1)
                for h in range(NHEAD):
                    for st in range(NST):
                        base = V_i(h, st, 0)
                        pb = (st % 2) * 20
                        ep1, ep2 = e1[st % 2], e2[st % 2]
                        V.wait_ge(a_ex, A_ek(h, st, 3))
                        V.tensor_tensor(out=pacc[:, pb + 8:pb + 9], in0=pacc[:, pb:pb + 1],
                                        in1=pacc[:, pb + 1:pb + 2], op=ALU.add).then_inc(v_cb, 1)
                        V.tensor_tensor(out=pacc[:, pb + 9:pb + 10], in0=pacc[:, pb + 2:pb + 3],
                                        in1=pacc[:, pb + 3:pb + 4], op=ALU.add).then_inc(v_cb, 1)
                        V.wait_ge(v_cb, base + 2)
                        V.tensor_tensor(out=pacc[:, pb + 10:pb + 11], in0=pacc[:, pb + 8:pb + 9],
                                        in1=pacc[:, pb + 9:pb + 10], op=ALU.add).then_inc(v_cb, 1)
                        V.wait_ge(v_cb, base + 3)
                        V.reciprocal(out=pacc[:, pb + 11:pb + 12],
                                     in_=pacc[:, pb + 10:pb + 11]).then_inc(v_cb, 1)
                        V.wait_ge(a_ex, A_ek(h, st, 7))
                        V.tensor_tensor(out=pacc[:, pb + 12:pb + 13], in0=pacc[:, pb + 4:pb + 5],
                                        in1=pacc[:, pb + 5:pb + 6], op=ALU.add).then_inc(v_cb, 1)
                        V.tensor_tensor(out=pacc[:, pb + 13:pb + 14], in0=pacc[:, pb + 6:pb + 7],
                                        in1=pacc[:, pb + 7:pb + 8], op=ALU.add).then_inc(v_cb, 1)
                        V.wait_ge(v_cb, base + 6)
                        V.tensor_tensor(out=pacc[:, pb + 14:pb + 15], in0=pacc[:, pb + 12:pb + 13],
                                        in1=pacc[:, pb + 13:pb + 14], op=ALU.add).then_inc(v_cb, 1)
                        V.wait_ge(v_cb, base + 7)
                        V.reciprocal(out=pacc[:, pb + 15:pb + 16],
                                     in_=pacc[:, pb + 14:pb + 15]).then_inc(v_cb, 1)
                        V.wait_ge(v_cb, base + 8)
                        V.tensor_tensor(out=pacc[:, pb + 16:pb + 17], in0=pacc[:, pb + 15:pb + 16],
                                        in1=lambn[:, h:h + 1], op=ALU.mult).then_inc(v_cb, 1)
                        V.wait_ge(v_cb, base + 9)
                        V.tensor_scalar_mul(out=t1e[:, :], in0=ep1[:, :],
                                            scalar1=pacc[:, pb + 11:pb + 12]).then_inc(v_cb, 1)
                        V.wait_ge(v_cb, base + 10)
                        if st >= 1:
                            V.wait_ge(t_sc, T_tr(h, st - 1, 3))
                        elif h >= 1:
                            V.wait_ge(t_sc, T_tr(h - 1, 15, 3))
                        V.scalar_tensor_tensor(out=dsb[:, :], in0=ep2[:, :],
                                               scalar=pacc[:, pb + 16:pb + 17], in1=t1e[:, :],
                                               op0=ALU.mult, op1=ALU.add).then_inc(v_cb, 1)
                V.wait_ge(a_ex, A2 + 17)
                V.reciprocal(out=RSTDr, in_=M1r).then_inc(v_cb, 1)
                V.wait_ge(v_cb, V2 + 1)
                V.tensor_scalar(out=VARr, in0=RSTDr, scalar1=sft[0:1, 1:2], scalar2=None,
                                op0=ALU.mult).then_inc(v_cb, 1)
                for c in range(NDC):
                    V.wait_ge(o2_ld[c % 2], 16 * (c // 2 + 1))
                    V.wait_ge(a_ex, A2 + 18)
                    V.scalar_tensor_tensor(out=xbf[:, c * NTOK:(c + 1) * NTOK],
                                           in0=obuf[c % 2][:, :], scalar=ns_s[:, c:c + 1],
                                           in1=Rsb[:, :], op0=ALU.mult,
                                           op1=ALU.mult).then_inc(v_cb, 1)
                for dt in range(NDC):
                    V.wait_ge(t_sc, T2 + 18 + dt)
                    V.wait_ge(xr_ld[dt % 2], 16 * (dt // 2 + 1))
                    if dt >= 2:
                        V.wait_ge(z3st[dt % 2], 16 * (dt // 2))
                    psD = (PS[:, 0:2048], PS[:, 2048:4096])[dt % 2]
                    V.tensor_tensor(out=z3sb[dt % 2][:, :], in0=psD[:, :],
                                    in1=xres[dt % 2][:, :], op=ALU.add).then_inc(v_cb, 1)
                V.wait_ge(a_ex, A2 + 35)
                V.tensor_tensor(out=VARr, in0=M1r, in1=M1r, op=ALU.mult).then_inc(v_cb, 1)
                V.wait_ge(v_cb, V2 + 35)
                V.tensor_tensor(out=VARr, in0=M2r, in1=VARr, op=ALU.subtract).then_inc(v_cb, 1)
                V.wait_ge(a_ex, A2 + 36)
                V.reciprocal(out=RSTDr, in_=SVARr).then_inc(v_cb, 1)
                V.wait_ge(v_cb, V2 + 37)
                V.tensor_tensor(out=NMBr, in0=M1r, in1=RSTDr, op=ALU.mult).then_inc(v_cb, 1)
                e1f = e1[0].ap().bitcast(F32)
                for c in range(NDC):
                    V.wait_ge(xr_ld[c % 2], 16 * (NDC // 2 + c // 2 + 1))
                    V.wait_ge(a_ex, A2 + 38)
                    if c >= 2:
                        V.wait_ge(ostx[c % 2], 16 * (c // 2))
                    V.tensor_tensor(out=e1f[:, :], in0=xres[c % 2][:, :], in1=Rsb[:, :],
                                    op=ALU.mult).then_inc(v_cb, 1)
                    V.wait_ge(v_cb, V2 + 38 + 2 * c + 1)
                    V.tensor_tensor(out=z3sb[c % 2][:, :], in0=e1f[:, :],
                                    in1=oq[:, :], op=ALU.subtract).then_inc(v_cb, 1)
    return [(ostx[0], 16 * 8), (ostx[1], 16 * 8)]



def kernel_unfused(**inputs):
    x0 = np.ascontiguousarray(np.asarray(inputs["L0_attn_x"]), dtype=np.float32)
    bv = np.asarray(inputs["L0_attn_block_values"])
    cols = np.asarray(inputs["L0_attn_cols"])
    rows = np.asarray(inputs["L0_attn_rows"])
    ident = np.eye(128, dtype=np.float32)

    side, slots, nslot = conv_plan(rows, cols)
    row_idx = {h: np.repeat(np.asarray(side[h], np.int64) * 64, 64)
               + np.tile(np.arange(64, dtype=np.int64), 32) for h in (0, 1)}

    nc_conv = build_conv(nslot)
    im1 = []
    for c in range(NCORE):
        b, h = c // 2, c % 2
        im1.append(conv_stage_half(x0[b], bv, cols, side[h], slots[h], nslot))
    r1 = run_bass_kernel_spmd(nc_conv, im1, core_ids=list(range(NCORE)))
    x1 = [r1.results[c]["x1fm"] for c in range(NCORE)]

    nc_ffn = build_ffn()
    im2 = ffn_in_maps(x1, inputs["L0_ffn_wq_gate"], inputs["L0_ffn_s_gate"],
                      inputs["L0_ffn_wq_up"], inputs["L0_ffn_s_up"],
                      inputs["L0_ffn_wq_down"], inputs["L0_ffn_s_down"])
    r2 = run_bass_kernel_spmd(nc_ffn, im2, core_ids=list(range(NCORE)))
    x2 = [r2.results[c]["yfm"] for c in range(NCORE)]

    nc_kv = build_kv()
    Wk = np.ascontiguousarray(np.asarray(inputs["L1_attn_Wk"], np.float32))
    Wv = np.ascontiguousarray(np.asarray(inputs["L1_attn_Wv"], np.float32))
    im3 = [{"xfm": x2[c], "Wk": Wk, "Wv": Wv} for c in range(NCORE)]
    r3 = run_bass_kernel_spmd(nc_kv, im3, core_ids=list(range(NCORE)))

    im4 = []
    Wq = np.ascontiguousarray(np.asarray(inputs["L1_attn_Wq"], np.float32))
    Wo = np.ascontiguousarray(np.asarray(inputs["L1_attn_Wo"], np.float32))
    nsT = np.ascontiguousarray(np.asarray(inputs["L1_attn_norm_scale"],
                                          np.float32).reshape(NDC, 128).T)
    lamr = np.asarray(inputs["L1_attn_lam"], np.float32).reshape(1, NHEAD)
    for c in range(NCORE):
        b = c // 2
        kf = np.concatenate([r3.results[2 * b]["kout"], r3.results[2 * b + 1]["kout"]], axis=1)
        vf = np.concatenate([r3.results[2 * b]["vout"], r3.results[2 * b + 1]["vout"]], axis=0)
        im4.append({"xfm": x2[c], "kfull": np.ascontiguousarray(kf),
                    "vfull": np.ascontiguousarray(vf), "Wq": Wq, "Wo": Wo,
                    "nsT": nsT, "lam": lamr, "idf": ident})

    nc_at = build_attn()
    r4 = run_bass_kernel_spmd(nc_at, im4, core_ids=list(range(NCORE)))
    x3 = [r4.results[c]["x3fm"] for c in range(NCORE)]

    im5 = ffn_in_maps(x3, inputs["L1_ffn_wq_gate"], inputs["L1_ffn_s_gate"],
                      inputs["L1_ffn_wq_up"], inputs["L1_ffn_s_up"],
                      inputs["L1_ffn_wq_down"], inputs["L1_ffn_s_down"])
    r5 = run_bass_kernel_spmd(nc_ffn, im5, core_ids=list(range(NCORE)))

    out = np.empty((4, 2 * NTOK, D), np.float32)
    for c in range(NCORE):
        b, h = c // 2, c % 2
        out[b, row_idx[h], :] = r5.results[c]["yfm"].T
    return out



def emit_xchg(nc, osem, kloc, vloc, sh_k, sh_v, sh_flag, tok, deps):
    xch = osem("Xdone")
    with contextlib.ExitStack() as ctx:
        tt = ctx.enter_context(nc.sbuf_tensor("Xtok", [1, 1], I32))
        ft = ctx.enter_context(nc.sbuf_tensor("Xflag", [1, 1], I32))
        ld = osem("Xld")
        ps_ = osem("Xps")
        block = ctx.enter_context(nc.Block())

        @block.gpsimd
        def _(g):
            for s, v in deps:
                g.wait_ge(s, v)
            g.dma_start(out=tt[:, :], in_=tok[:, :]).then_inc(ld, 16)
            g.wait_ge(ld, 16)
            pid = nc.partition_id(engines=[mybir.EngineType.Pool])
            with g.register("Xpar") as par:
                g.reg_alu(par, pid, 1, ALU.bitwise_and)
                with g.If_eq(par, 0):
                    g.dma_start(out=sh_k[0], in_=kloc[:, :]).then_inc(ld, 16)
                    g.dma_start(out=sh_v[0], in_=vloc[:, :]).then_inc(ld, 16)
                    g.wait_ge(ld, 48)
                    g.dma_start(out=sh_flag[0:1, :], in_=tt[:, :]).then_inc(ld, 16)
                with g.Else():
                    g.dma_start(out=sh_k[1], in_=kloc[:, :]).then_inc(ld, 16)
                    g.dma_start(out=sh_v[1], in_=vloc[:, :]).then_inc(ld, 16)
                    g.wait_ge(ld, 48)
                    g.dma_start(out=sh_flag[1:2, :], in_=tt[:, :]).then_inc(ld, 16)
                g.wait_ge(ld, 64)
                with (
                    g.register("Xtgt") as tgt,
                    g.register("Xexp") as expreg,
                    g.register("Xobs") as obs,
                    g.register("Xneq") as neq,
                ):
                    g.reg_load(expreg, tt[0:1, 0:1])
                    g.reg_mov(tgt, 16)

                    def poll():
                        with g.If_eq(par, 0):
                            g.dma_start(out=ft[:, :], in_=sh_flag[1:2, :]).then_inc(ps_, 16)
                        with g.Else():
                            g.dma_start(out=ft[:, :], in_=sh_flag[0:1, :]).then_inc(ps_, 16)
                        g.wait_ge(ps_, tgt)
                        g.reg_add(tgt, tgt, 16)
                        g.reg_load(obs, ft[0:1, 0:1])
                        g.reg_alu(neq, obs, expreg, ALU.not_equal)

                    poll()
                    with g.While(neq):
                        poll()
            g.engine_nop().then_inc(xch, 1)
    return [(xch, 1)]



def build_fused(nslot, upto="full"):
    nc = bass.Bass("TRN2", target_bir_lowering=False, debug=False)
    ap = lambda name, shape, dt: nc.dram_tensor(name, shape, dt, kind="ExternalInput").ap()
    x0 = ap("x0", [NTOK, D], F32)
    xsrc = ap("xsrc", [nslot, 64, D], F32)
    wsl = ap("wsl", [nslot, 128, 512], F32)
    idx = ap("idx", [64, nslot], I32)
    idf = ap("idf", [128, 128], F32)
    wgT0 = ap("wgT0", [D, HID], I32)
    wuT0 = ap("wuT0", [D, HID], I32)
    wdT0 = ap("wdT0", [HID, D], I32)
    sgT0 = ap("sgT0", [128, NHC], F32)
    suT0 = ap("suT0", [128, NHC], F32)
    sdT0 = ap("sdT0", [128, NDC], F32)
    wgT1 = ap("wgT1", [D, HID], I32)
    wuT1 = ap("wuT1", [D, HID], I32)
    wdT1 = ap("wdT1", [HID, D], I32)
    sgT1 = ap("sgT1", [128, NHC], F32)
    suT1 = ap("suT1", [128, NHC], F32)
    sdT1 = ap("sdT1", [128, NDC], F32)
    Wq = ap("Wq", [D, 2 * DH * NHEAD], F32)
    Wk = ap("Wk", [D, 2 * DH * NHEAD], F32)
    Wv = ap("Wv", [D, D], F32)
    Wo = ap("Wo", [D, D], F32)
    nsT = ap("nsT", [128, NDC], F32)
    lam = ap("lam", [1, NHEAD], F32)
    tok = ap("tok", [1, 1], I32)
    yfm = nc.dram_tensor("yfm", [D, NTOK], F32, kind="ExternalOutput").ap()

    zdr = nc.dram_tensor("zdr", [NTOK, D], F32).ap()
    x1 = nc.dram_tensor("x1", [D, NTOK], F32).ap()
    x2 = nc.dram_tensor("x2", [D, NTOK], F32).ap()
    x3 = nc.dram_tensor("x3", [D, NTOK], F32).ap()
    zscr0 = nc.dram_tensor("zscr0", [D, NTOK], F32).ap()
    zscr1 = nc.dram_tensor("zscr1", [D, NTOK], F32).ap()
    kloc = nc.dram_tensor("kloc", [2 * DH * NHEAD, NTOK], BF16).ap()
    vloc = nc.dram_tensor("vloc", [NTOK, D], BF16).ap()
    oscr = nc.dram_tensor("oscr", [D, NTOK], BF16).ap()
    sh_kt = nc.dram_tensor("sh_k", [2, 2 * DH * NHEAD, NTOK], BF16, addr_space="Shared").ap()
    sh_vt = nc.dram_tensor("sh_v", [2, NTOK, D], BF16, addr_space="Shared").ap()
    sh_flag = nc.dram_tensor("sh_fl", [2, 1], I32, addr_space="Shared").ap()

    with contextlib.ExitStack() as octx:
        def osem(name):
            return octx.enter_context(nc.semaphore(name))

        if upto == "conv":
            emit_conv(nc, osem, "C", nslot, x0, xsrc, wsl, idx, idf, yfm, zdr, [])
            return nc
        ev = emit_conv(nc, osem, "C", nslot, x0, xsrc, wsl, idx, idf, x1, zdr, [])
        if upto == "f0":
            emit_ffn(nc, osem, "F0", x1, wgT0, wuT0, wdT0, sgT0, suT0,
                     sdT0, yfm, zscr0, ev)
            return nc
        ev, fbank = emit_ffn(nc, osem, "F0", x1, wgT0, wuT0, wdT0, sgT0, suT0,
                             sdT0, x2, zscr0, ev)
        ev_kv = emit_kv(nc, osem, "K", x2, Wk, Wv, kloc, vloc, ev)
        ev_x = emit_xchg(nc, osem, kloc, vloc,
                         [sh_kt[0], sh_kt[1]], [sh_vt[0], sh_vt[1]],
                         sh_flag, tok, ev_kv)
        if upto == "xchg":
            with contextlib.ExitStack() as c2:
                dn = octx.enter_context(nc.semaphore("XOUT"))
                blk = c2.enter_context(nc.Block())

                @blk.gpsimd
                def _(g):
                    for s, v in ev_x:
                        g.wait_ge(s, v)
                    g.dma_start(out=yfm[0:D, 0:NTOK // 2],
                                in_=sh_kt[0][0:D, 0:NTOK // 2]).then_inc(dn, 16)
                    g.dma_start(out=yfm[0:D, NTOK // 2:NTOK],
                                in_=sh_kt[1][0:D, 0:NTOK // 2]).then_inc(dn, 16)
                    g.wait_ge(dn, 32)
            return nc
        kv_srcs = ([sh_kt[0], sh_kt[1]], [sh_vt[0], sh_vt[1]])
        ev = emit_attn(nc, osem, "A", x2, kv_srcs, Wq, Wo, nsT, lam, idf,
                       x3, oscr, zscr1, ev, kv_gate=ev_x)
        if upto == "attn":
            with contextlib.ExitStack() as c2:
                dn = octx.enter_context(nc.semaphore("XOUT"))
                blk = c2.enter_context(nc.Block())

                @blk.gpsimd
                def _(g):
                    for s, v in ev:
                        g.wait_ge(s, v)
                    g.dma_start(out=yfm[:, :], in_=x3[:, :]).then_inc(dn, 16)
                    g.wait_ge(dn, 16)
            return nc
        emit_ffn(nc, osem, "F1", x3, wgT1, wuT1, wdT1, sgT1, suT1, sdT1,
                 yfm, zscr1, ev, sem_bank=fbank)
    return nc


def _ffn_w(inputs, layer):
    wgT = np.ascontiguousarray(np.asarray(inputs[f"L{layer}_ffn_wq_gate"]).T)
    wuT = np.ascontiguousarray(np.asarray(inputs[f"L{layer}_ffn_wq_up"]).T)
    wdT = np.ascontiguousarray(np.asarray(inputs[f"L{layer}_ffn_wq_down"]).T)
    sgT = np.ascontiguousarray(np.asarray(inputs[f"L{layer}_ffn_s_gate"], np.float32).reshape(NHC, 128).T)
    suT = np.ascontiguousarray(np.asarray(inputs[f"L{layer}_ffn_s_up"], np.float32).reshape(NHC, 128).T)
    sdT = np.ascontiguousarray(np.asarray(inputs[f"L{layer}_ffn_s_down"], np.float32).reshape(NDC, 128).T)
    return wgT, wuT, wdT, sgT, suT, sdT


def kernel_fused(upto="full", **inputs):
    x0 = np.ascontiguousarray(np.asarray(inputs["L0_attn_x"]), dtype=np.float32)
    bv = np.asarray(inputs["L0_attn_block_values"])
    cols = np.asarray(inputs["L0_attn_cols"])
    rows = np.asarray(inputs["L0_attn_rows"])

    side, slots, nslot = conv_plan(rows, cols)
    row_idx = {h: np.repeat(np.asarray(side[h], np.int64) * 64, 64)
               + np.tile(np.arange(64, dtype=np.int64), 32) for h in (0, 1)}

    w0 = _ffn_w(inputs, 0)
    w1 = _ffn_w(inputs, 1)
    Wq = np.ascontiguousarray(np.asarray(inputs["L1_attn_Wq"], np.float32))
    Wk = np.ascontiguousarray(np.asarray(inputs["L1_attn_Wk"], np.float32))
    Wv = np.ascontiguousarray(np.asarray(inputs["L1_attn_Wv"], np.float32))
    Wo = np.ascontiguousarray(np.asarray(inputs["L1_attn_Wo"], np.float32))
    nsT = np.ascontiguousarray(np.asarray(inputs["L1_attn_norm_scale"],
                                          np.float32).reshape(NDC, 128).T)
    lamr = np.asarray(inputs["L1_attn_lam"], np.float32).reshape(1, NHEAD)
    tokv = np.array([[np.random.default_rng().integers(1, 2**31 - 1)]], np.int32)

    nc = build_fused(nslot, upto)
    im = []
    for c in range(NCORE):
        b, h = c // 2, c % 2
        m = conv_stage_half(x0[b], bv, cols, side[h], slots[h], nslot)
        m.update({
            "wgT0": w0[0], "wuT0": w0[1], "wdT0": w0[2],
            "sgT0": w0[3], "suT0": w0[4], "sdT0": w0[5],
            "wgT1": w1[0], "wuT1": w1[1], "wdT1": w1[2],
            "sgT1": w1[3], "suT1": w1[4], "sdT1": w1[5],
            "Wq": Wq, "Wk": Wk, "Wv": Wv, "Wo": Wo,
            "nsT": nsT, "lam": lamr, "tok": tokv,
        })
        im.append(m)
    res = run_bass_kernel_spmd(nc, im, core_ids=list(range(NCORE)))
    out = np.empty((4, 2 * NTOK, D), np.float32)
    for c in range(NCORE):
        b, h = c // 2, c % 2
        out[b, row_idx[h], :] = res.results[c]["yfm"].T
    return out


def kernel(**inputs):
    return kernel_fused(**inputs)
